# version 09
# baseline (speedup 1.0000x reference)
import sys
sys.path.insert(0, '/opt/trn_rl_repo')
import contextlib
import numpy as np
import ml_dtypes
import concourse.bass as bass
import concourse.mybir as mybir
from concourse.bass_utils import run_bass_kernel_spmd

F32 = mybir.dt.float32
BF16 = mybir.dt.bfloat16
AF = mybir.ActivationFunctionType
ALU = mybir.AluOpType
AX = mybir.AxisListType

NCORES = 8
B = 2
T = 2048
D = 1024
NH = 16
FF = 4096
NT = (B * T) // 128
MAGIC = 1.5 * 2.0 ** 23
LN_EPS = 1e-5

PAIRMASK = []
for _i in range(0, 32, 2):
    PAIRMASK += [_i + 1, _i]

ALL_PHASES = ("l0attn", "l0ffn", "l1attn", "l1ffn")
L1_LIM = None
L1_STAGE = 99
L1_SKIP_SCAN = False
L1_SKIP_CARRY = False
L1_CARRY_MODE = "full"
L1_CARRY_OUT = "E"
L1_SKIP_QV = False
L1_SKIP_TR = False
L1P_SKIP = set()


def host_constants():
    c = {}
    c["c_ident"] = np.eye(128, dtype=ml_dtypes.bfloat16)
    inv_freq = (1.0 / 10000.0 ** (np.arange(0, 64, 2, dtype=np.float64) / 64))
    t = np.arange(T, dtype=np.float64)
    ang = t[None, :] * inv_freq[:, None]
    cos64 = np.repeat(np.cos(ang), 2, axis=0)
    sin64 = np.repeat(np.sin(ang), 2, axis=0)
    sgn = np.where(np.arange(64) % 2 == 0, -1.0, 1.0)[:, None]
    sinS = sin64 * sgn
    cos128 = np.concatenate([cos64, cos64], 0).astype(np.float32)
    sin128 = np.concatenate([sinS, sinS], 0).astype(np.float32)
    c["c_ropeqc"] = cos128
    c["c_ropeqs"] = sin128
    c["c_ropekc"] = (cos128 / 8.0).astype(np.float32)
    c["c_ropeks"] = (sin128 / 8.0).astype(np.float32)
    w = np.arange(1024)[None, :]
    p_ = np.arange(128)[:, None]
    c["c_bigm"] = (w >= p_ + 512).astype(ml_dtypes.bfloat16)
    i = np.arange(128)[None, :]
    j = np.arange(128)[:, None]
    c["c_tri01"] = (i >= j).astype(ml_dtypes.bfloat16)
    c["c_iota"] = np.arange(128, dtype=np.float32)[None, :]
    c["c_iota1"] = (np.arange(128, dtype=np.float32) + 1.0)[None, :]
    c["c_ones_bf"] = np.ones((128, 2), dtype=ml_dtypes.bfloat16)
    c["c_ones_f32"] = np.ones((128, 2), dtype=np.float32)
    c["c_onesrow_bf"] = np.ones((1, 128), dtype=ml_dtypes.bfloat16)
    c["c_onesrow_f32"] = np.ones((1, 128), dtype=np.float32)
    c["c_ident4f"] = np.eye(4, dtype=np.float32)
    c["c_magic"] = np.full((128, 1), 1.5 * 2.0 ** 23, dtype=np.float32)
    c["c_iotar"] = (127.0 - np.arange(128, dtype=np.float32))[None, :]
    return c


class Ctr:
    def __init__(self, h):
        self.h = h
        self.n = 0


class P:
    ENGINES = ("sync", "tensor", "vector", "scalar", "gpsimd")

    def __init__(self, nc, ctx):
        self.nc = nc
        self.items = {e: [] for e in self.ENGINES}
        self._waited = {}
        self._ctrs = {}
        self._ctx = ctx

    def ctr(self, name):
        if name not in self._ctrs:
            h = self._ctx.enter_context(self.nc.semaphore(name))
            self._ctrs[name] = Ctr(h)
        return self._ctrs[name]

    def add(self, eng, fn=None, wait=(), inc=None, amt=1):
        w = []
        for (c, v_) in wait:
            if v_ <= 0:
                continue
            key = (eng, id(c))
            if self._waited.get(key, -1) >= v_:
                continue
            self._waited[key] = v_
            w.append((c, v_))
        if inc is not None:
            inc.n += amt
        self.items[eng].append((fn, w, inc, amt))

    def emit(self):
        with self.nc.Block() as block:
            def mk(eng_items):
                def run(eng):
                    for fn, waits, inc, amt in eng_items:
                        for (c, v_) in waits:
                            eng.wait_ge(c.h, v_)
                        ins = fn(eng) if fn is not None else None
                        if inc is not None:
                            if ins is None:
                                ins = eng.nop()
                            ins.then_inc(inc.h, amt)
                return run
            block.sync(mk(self.items["sync"]))
            block.tensor(mk(self.items["tensor"]))
            block.vector(mk(self.items["vector"]))
            block.scalar(mk(self.items["scalar"]))
            block.gpsimd(mk(self.items["gpsimd"]))


class Arena:
    def __init__(self, nc, nbytes):
        self.nc = nc
        self.base_t = nc.alloc_sbuf_tensor("arena", [128, nbytes],
                                           mybir.dt.uint8)
        self.base_addr = nc.lookup_mloc(self.base_t).addr
        self.off = 0
        self.nbytes = nbytes
        self.stack = []
        self.count = 0
        self.peak = 0

    def alloc(self, shape, dtype, name=None):
        per_part = int(np.prod(shape[1:])) * mybir.dt.size(dtype)
        a = (self.off + 63) // 64 * 64
        assert a + per_part <= self.nbytes, \
            f"arena overflow: {a}+{per_part} > {self.nbytes} ({name})"
        self.off = a + per_part
        self.peak = max(self.peak, self.off)
        self.count += 1
        t = self.nc.alloc_sbuf_tensor_at(name or f"ar{self.count}",
                                         list(shape), dtype,
                                         offset=self.base_addr + a)
        return t

    def mark(self):
        self.stack.append(self.off)

    def release(self):
        self.off = self.stack.pop()


def build_nc(phases=ALL_PHASES, dump_res=False):
    nc = bass.Bass("TRN2", target_bir_lowering=False,
                   detect_race_conditions=False)
    ctx = contextlib.ExitStack()

    def din(name, shape, dtype=F32):
        return nc.dram_tensor(name, list(shape), dtype,
                              kind="ExternalInput").ap()

    x_in = din("x", [B * T, D])
    w_qkv = din("L0_attn_Wqkv", [D, 3 * D])
    w_proj0 = din("L0_attn_Wproj", [D, D])
    w_fc = din("L0_ffn_W_fc", [FF, D])
    ln_w = din("L0_ffn_ln_w", [FF])
    ln_b = din("L0_ffn_ln_b", [FF])
    w_pr0 = din("L0_ffn_W_proj", [D, FF])
    dec_in = din("L1_attn_decay", [NH])
    w_q1 = din("L1_attn_Wq", [D, D])
    w_v1 = din("L1_attn_Wv", [D, D])
    w_o1 = din("L1_attn_Wo", [D, D])
    w_g1 = din("L1_attn_Wg", [D, D])
    gam1 = din("L1_attn_ln_gamma", [D])
    bet1 = din("L1_attn_ln_beta", [D])
    w_up = din("L1_ffn_w_up", [FF, D])
    w_dn = din("L1_ffn_w_down", [D, FF])
    c_ident = din("c_ident", [128, 128], BF16)
    c_rqc = din("c_ropeqc", [128, T])
    c_rqs = din("c_ropeqs", [128, T])
    c_rkc = din("c_ropekc", [128, T])
    c_rks = din("c_ropeks", [128, T])
    c_bigm = din("c_bigm", [128, 1024], BF16)
    c_tri01 = din("c_tri01", [128, 128], BF16)
    c_iota = din("c_iota", [1, 128])
    c_iota1 = din("c_iota1", [1, 128])
    c_ones_bf = din("c_ones_bf", [128, 2], BF16)
    c_ones_f32 = din("c_ones_f32", [128, 2])
    c_orow_bf = din("c_onesrow_bf", [1, 128], BF16)
    c_orow_f32 = din("c_onesrow_f32", [1, 128])
    c_id4f = din("c_ident4f", [4, 4])
    c_magic = din("c_magic", [128, 1])
    c_iotar = din("c_iotar", [1, 128])

    out_ap = nc.dram_tensor("out", [B * T, D], F32, kind="ExternalOutput").ap()
    if "attdbg" in phases:
        dbg3_ap = nc.dram_tensor("dbg3", [4, 128, 4096], F32,
                                 kind="ExternalOutput").ap()
    else:
        dbg3_ap = None
    if "ffndbg" in phases:
        dbg2_ap = nc.dram_tensor("dbg2", [4, 128, 4096], F32,
                                 kind="ExternalOutput").ap()
        dbg4_ap = nc.dram_tensor("dbg4", [D, FF], BF16,
                                 kind="ExternalOutput").ap()
    else:
        dbg2_ap = None
        dbg4_ap = None
    if dump_res or "l1ffn" not in phases:
        resd_ap = nc.dram_tensor("res_dump", [B * T, D], F32,
                                 kind="ExternalOutput").ap()
    else:
        resd_ap = None

    kscr = nc.dram_tensor("kscr", [B, 8, 128, T], BF16, kind="Internal").ap()
    qscr = nc.dram_tensor("qscr", [B, 8, 128, T], BF16, kind="Internal").ap()
    vscr = nc.dram_tensor("vscr", [B, 16, 128, 1040], BF16,
                          kind="Internal").ap()
    w1t_scr = [nc.dram_tensor(f"w1t{i}", [D, FF], BF16, kind="Internal").ap()
               for i in range(2)]

    res_t = nc.alloc_sbuf_tensor("res", [128, NT * D], BF16)
    ident_t = nc.alloc_sbuf_tensor("ident", [128, 128], BF16)
    ones_bf = nc.alloc_sbuf_tensor("onesbf", [128, 2], BF16)
    ones_f32 = nc.alloc_sbuf_tensor("onesf32", [128, 2], F32)
    orow_bf = nc.alloc_sbuf_tensor("orowbf", [1, 128], BF16)
    orow_f32 = nc.alloc_sbuf_tensor("orowf32", [1, 128], F32)
    iota_t = nc.alloc_sbuf_tensor("iota", [1, 128], F32)
    iota1_t = nc.alloc_sbuf_tensor("iota1", [1, 128], F32)
    precols = nc.alloc_sbuf_tensor("precols", [128, NT * 2], F32)
    tcols = nc.alloc_sbuf_tensor("tcols", [128, 64], F32)
    bnsc = nc.alloc_sbuf_tensor("bnsc", [128, 12], F32)
    id4f_t = nc.alloc_sbuf_tensor("id4f", [4, 4], F32)
    magic_t = nc.alloc_sbuf_tensor("magic", [128, 1], F32)
    iotar_t = nc.alloc_sbuf_tensor("iotar", [1, 128], F32)

    psA = nc.alloc_psum_tensor("psA", [128, 1024], F32)
    psB = nc.alloc_psum_tensor("psB", [128, 1024], F32)
    psC = nc.alloc_psum_tensor("psC", [128, 512], F32)
    psD = nc.alloc_psum_tensor("psD", [128, 512], F32)
    ptA = nc.alloc_psum_tensor("ptA", [128, 1024], BF16)
    psE = nc.alloc_psum_tensor("psE", [128, 512], F32)

    avail = nc.sbuf_bytes_remaining - 128
    ar = Arena(nc, avail)

    p = P(nc, ctx)
    tc = p.ctr("tc")
    vc = p.ctr("vc")
    sc = p.ctr("sc")
    dC0 = p.ctr("dC0")
    dX = p.ctr("dX")
    dOUT = p.ctr("dOUT")
    dDBG = p.ctr("dDBG")
    dDBG2 = p.ctr("dDBG2")

    def t_mm(out, lhsT, rhs, start, stop, wait=(), inc=None):
        p.add("tensor", lambda e, o=out, l=lhsT, r=rhs, s_=start, q=stop:
              e.matmul(o, l, r, start=s_, stop=q), wait=wait, inc=inc)

    ptEb = psE[:, :].bitcast(BF16)

    def t_tr(out, in_, wait=(), inc=None):
        p.add("tensor", lambda e, o=out, i=in_:
              e.transpose(o, i, ident_t[:, :]), wait=wait, inc=inc)

    def tr8(src, wait=()):
        w0 = list(wait) + bwait("T") + bwait("E")
        for k in range(8):
            tgt = ptA if k % 2 == 0 else ptEb
            p.add("tensor",
                  lambda e, k=k, tgt=tgt, src=src: e.transpose(
                      tgt[:, k * 128:(k + 1) * 128],
                      src[:, k * 128:(k + 1) * 128], ident_t[:, :]),
                  wait=w0 if k == 0 else (),
                  inc=tc if k == 7 else None)
        return tc.n

    def evac_pair(dst_even, dst_odd, trN, inc=None):
        pa = ptA[:, :].rearrange("p (a two c2) -> p a two c2", a=4, two=2)
        pe = ptEb[:, :].rearrange("p (a two c2) -> p a two c2", a=4, two=2)
        v(lambda e: e.tensor_copy(dst_even, pa[:, :, 0, :]),
          wait=[(tc, trN)])
        v(lambda e: e.tensor_copy(dst_odd, pe[:, :, 1, :]), inc=inc or vc)
        bset("T", inc or vc)
        bset("E", inc or vc)

    def v(fn, wait=(), inc=None):
        p.add("vector", fn, wait=wait, inc=inc)

    def s(fn, wait=(), inc=None):
        p.add("scalar", fn, wait=wait, inc=inc)

    def dma(eng, out, in_, wait=(), inc=None, slow=False):
        if slow:
            p.add(eng, lambda e, o=out, i=in_: e.dma_start(
                o, i, allow_slow_non_contiguous=True), wait=wait,
                inc=inc, amt=16)
        else:
            p.add(eng, lambda e, o=out, i=in_: e.dma_start(o, i), wait=wait,
                  inc=inc, amt=16)

    bankg = {k: (None, 0) for k in "ABCDET"}

    def bwait(bk):
        c, n0 = bankg[bk]
        return [(c, n0)] if c is not None else []

    def bset(bk, c):
        bankg[bk] = (c, c.n)

    def ln_stats(src, width, c0, wait=(), lead=True):
        nch = (width + 511) // 512
        if lead:
            v(lambda e: e.drain())
        for k in range(nch):
            v(lambda e, sa=src, a=k * 512, b_=min(width, k * 512 + 512), kk=k:
              e.bn_stats(bnsc[:, kk * 6:kk * 6 + 6], sa[:, a:b_]),
              wait=wait if k == 0 else ())
        v(lambda e: e.drain())
        v(lambda e, n=nch, c=c0:
          e.bn_aggr(tcols[:, c:c + 2], bnsc[:, 0:n * 6]))
        v(lambda e: e.drain())

    _rstd_ln_sc = [0]

    def rstd_from_var(c_var, c_out, eps=LN_EPS, lead=False):
        if lead:
            v(lambda e: e.drain())
        v(lambda e: e.tensor_scalar(tcols[:, 62:63],
                                    tcols[:, c_var:c_var + 1], float(eps),
                                    None, ALU.add),
          wait=[(sc, _rstd_ln_sc[0])], inc=vc)
        v(lambda e: e.drain())
        v(lambda e: e.reciprocal(tcols[:, 63:64], tcols[:, 62:63]))
        v(lambda e: e.drain(), inc=vc)
        n = vc.n
        s(lambda e: e.activation(tcols[:, c_out:c_out + 1],
                                 tcols[:, 63:64], AF.Sqrt), wait=[(vc, n)],
          inc=sc)
        _rstd_ln_sc[0] = sc.n
        m = sc.n
        return m

    pair0 = ("l0attn" in phases) and ("l0ffn" in phases) and \
        ("xnonly" not in phases) and ("l0proj" not in phases)
    pair1 = ("l1attn" in phases) and ("l1ffn" in phases)
    eblock_on = pair0 or pair1
    early = None
    early1 = None
    eb_addr = None
    if eblock_on:
        ebufs = [ar.alloc([128, 1024], F32, f"ebuf{j}") for j in range(4)]
        eb_addr = [nc.lookup_mloc(t).addr for t in ebufs]
        assert eb_addr[1] == eb_addr[0] + 4096 and \
            eb_addr[3] == eb_addr[2] + 4096
        dEB = [p.ctr(f"dEB{j}") for j in range(4)]
        eaccs = ar.alloc([128, 4], F32, "eaccs")
        _eacc_off = [0]

        def mk_pump(w1_ap, w2_ap, acc_name):
            e_acc = eaccs[:, _eacc_off[0]:_eacc_off[0] + 2]
            _eacc_off[0] += 2
            tiles = [(w1_ap, r * 128, 0, 0) for r in range(32)] + \
                [(w2_ap, dt * 128, q * 1024, 1)
                 for dt in range(8) for q in range(4)]
            st = {"grp": 0, "cons_vc": 0, "pending": None,
                  "acc": e_acc, "addr": eb_addr}

            def start():
                v(lambda e: e.memset(e_acc[:, :], 0.0), inc=vc)
                st["cons_vc"] = vc.n

            def pump():
                if st["pending"] is not None:
                    for j, (nn, col) in enumerate(st["pending"]):
                        for k in range(2):
                            v(lambda e, j=j, k=k: e.tensor_reduce(
                                tcols[:, 8 + k:9 + k],
                                ebufs[j][:, k * 512:(k + 1) * 512],
                                AX.X, ALU.add, apply_absolute_value=True),
                              wait=[(dEB[j], nn)] if k == 0 else ())
                        v(lambda e: e.drain())
                        v(lambda e: e.tensor_tensor(
                            tcols[:, 8:9], tcols[:, 8:9], tcols[:, 9:10],
                            ALU.add))
                        v(lambda e: e.drain())
                        v(lambda e, col=col, e_acc=e_acc: e.tensor_tensor(
                            e_acc[:, col:col + 1], e_acc[:, col:col + 1],
                            tcols[:, 8:9], ALU.add), inc=vc)
                    st["cons_vc"] = vc.n
                    st["pending"] = None
                g = st["grp"]
                if g < 16:
                    pend = []
                    for j in range(4):
                        wsrc, r0, c0, col = tiles[g * 4 + j]
                        dma("gpsimd", ebufs[j][:, :],
                            wsrc[r0:r0 + 128, c0:c0 + 1024],
                            wait=[(vc, st["cons_vc"])] if j == 0 else (),
                            inc=dEB[j])
                        pend.append((dEB[j].n, col))
                    st["pending"] = pend
                    st["grp"] = g + 1
            st["pump"] = pump
            st["start"] = start
            return st
        if pair0:
            early = mk_pump(w_fc, w_pr0, "eacc0")
            early["start"]()

    for src, tgt in ((c_ident, ident_t), (c_ones_bf, ones_bf),
                     (c_ones_f32, ones_f32), (c_orow_bf, orow_bf),
                     (c_orow_f32, orow_f32), (c_iota, iota_t),
                     (c_iota1, iota1_t), (c_id4f, id4f_t),
                     (c_magic, magic_t), (c_iotar, iotar_t)):
        dma("sync", tgt[:, :], src[:, :], inc=dC0)
    for g in range(NT):
        dma("gpsimd", res_t[:, g * D:(g + 1) * D],
            x_in[g * 128:(g + 1) * 128, :], inc=dX)
    x_fin = dX.n

    l0attn_on = ("l0attn" in phases) or ("l0proj" in phases) or \
        ("xnonly" in phases)
    attn_sub = "l0attn" in phases
    if l0attn_on:
        ar.mark()
        wproj_b = ar.alloc([128, 8 * D], BF16, "wprojb")
        mbias = ar.alloc([128, 32], F32, "mbias")
        m2q = ar.alloc([1, 16], F32, "m2q")
        m2k = ar.alloc([1, 16], F32, "m2k")
        fscr = ar.alloc([128, 1024], F32, "fscr")
        bigm_t = ar.alloc([128, 1024], BF16, "bigm")
        dWP = p.ctr("dWP")
        dma("sync", bigm_t[:, :], c_bigm[:, :], inc=dC0)
        for kt in range(8):
            dma("gpsimd", wproj_b[:, kt * D:(kt + 1) * D],
                w_proj0[kt * 128:(kt + 1) * 128, :], inc=dWP)
        wp_fin = dWP.n

        v(lambda e: e.memset(m2q[:, :], 0.0))
        v(lambda e: e.memset(m2k[:, :], 0.0), inc=vc)

        ar.mark()
        ropeqc = ar.alloc([128, T], F32, "ropeqc")
        ropeqs = ar.alloc([128, T], F32, "ropeqs")
        ropekc = ar.alloc([128, T], F32, "ropekc")
        ropeks = ar.alloc([128, T], F32, "ropeks")
        wqkv_b = ar.alloc([128, 8 * 3 * D], BF16, "wqkvb")
        xnT = ar.alloc([128, 8 * 512], BF16, "xnT")
        xn_stage = ar.alloc([128, D], BF16, "xnstage")
        raw_s = ar.alloc([128, 512], BF16, "raws")
        sh_s = ar.alloc([128, 512], BF16, "shs")
        t1_s = ar.alloc([128, 512], BF16, "t1s")
        t2_s = ar.alloc([128, 512], BF16, "t2s")
        rope_o = [ar.alloc([128, 512], BF16, f"ropeo{i}") for i in range(2)]
        sq_s = ar.alloc([128, 512], BF16, "sqs")
        vstage = [ar.alloc([128, 1040], BF16, f"vstg{i}") for i in range(2)]

        dWQ = p.ctr("dWQ")
        dSpro = [p.ctr("dSpro0"), p.ctr("dSpro1")]
        dSpv = [p.ctr("dSpv0"), p.ctr("dSpv1")]
        for src, tgt in ((c_rqc, ropeqc), (c_rqs, ropeqs), (c_rkc, ropekc),
                         (c_rks, ropeks)):
            dma("sync", tgt[:, :], src[:, :], inc=dC0)
        for kt in range(8):
            dma("gpsimd", wqkv_b[:, kt * 3 * D:(kt + 1) * 3 * D],
                w_qkv[kt * 128:(kt + 1) * 128, :], inc=dWQ)
        wq_fin = dWQ.n

        ro_spill = {}
        vst_spill = {}
        mbias_vc = {}

        def proj_stats(b, c):
            for i in range(4):
                g = b * 16 + c * 4 + i
                ln_stats(res_t[:, g * D:(g + 1) * D], D, 0,
                         wait=[(dX, x_fin)])
                rn = rstd_from_var(1, 2)
                v(lambda e, g=g: e.tensor_copy(
                    precols[:, g * 2:g * 2 + 1], tcols[:, 0:1]),
                  wait=[(sc, rn)])
                v(lambda e, g=g: e.tensor_copy(
                    precols[:, g * 2 + 1:g * 2 + 2], tcols[:, 2:3]),
                  inc=vc)

        pseq = [(b, c) for b in ([0] if "xnonly" in phases else range(B))
                for c in ([0] if "xnonly" in phases else range(4))]
        proj_stats(*pseq[0])
        for pidx, (b, c) in enumerate(pseq):
            if True:
                for i in range(4):
                    g = b * 16 + c * 4 + i
                    v(lambda e, g=g: e.tensor_scalar(
                        xn_stage[:, :], res_t[:, g * D:(g + 1) * D],
                        precols[:, g * 2:g * 2 + 1],
                        precols[:, g * 2 + 1:g * 2 + 2],
                        ALU.subtract, ALU.mult), inc=vc)
                    xn_vc = vc.n
                    trN = tr8(xn_stage,
                              wait=[(vc, xn_vc), (dC0, dC0.n)])
                    xv4 = xnT[:, :].rearrange(
                        "p (a two c2) -> p a two c2", a=4, two=2)
                    evac_pair(xv4[:, :, 0, i * 128:(i + 1) * 128],
                              xv4[:, :, 1, i * 128:(i + 1) * 128], trN)
                xnT_vc = vc.n

                if dbg3_ap is not None and b == 0 and c == 0:
                    dma("gpsimd", dbg3_ap[0, :, 0:4096], xnT[:, :],
                        wait=[(vc, xnT_vc)], inc=dDBG2)
                if "xnonly" in phases:
                    continue
                    for w_ in range(3):
                        dma("gpsimd", dbg3_ap[1 + w_, :, 0:4096],
                            wqkv_b[:, w_ * 4096:(w_ + 1) * 4096],
                            wait=[(dWQ, wq_fin)], inc=dDBG2)
                for kind, colbase, tabc, tabs, scrap, m2 in (
                        ("q", 0, ropeqc, ropeqs, qscr, m2q),
                        ("k", D, ropekc, ropeks, kscr, m2k)):
                    for et in range(8):
                        ps, bk = (psC, "C") if et % 2 == 0 else (psD, "D")
                        for dt in range(8):
                            t_mm(ps[:, :],
                                 wqkv_b[:, dt * 3 * D + colbase + et * 128:
                                        dt * 3 * D + colbase + (et + 1) * 128],
                                 xnT[:, dt * 512:(dt + 1) * 512],
                                 start=(dt == 0), stop=(dt == 7),
                                 wait=(bwait(bk) +
                                       [(vc, xnT_vc), (dWQ, wq_fin),
                                        (dC0, dC0.n)])
                                 if dt == 0 else (),
                                 inc=tc if dt == 7 else None)
                        mmN = tc.n
                        wrs = ([(vc, ro_spill.get("rope_vc", 0))]
                               if ro_spill.get("rope_vc") else [])
                        s(lambda e, ps=ps: e.copy(raw_s[:, :], ps[:, :]),
                          wait=[(tc, mmN)] + wrs, inc=sc)
                        s(lambda e, ps=ps: e.square(sq_s[:, :], ps[:, :]),
                          inc=sc)
                        sqN = sc.n
                        v(lambda e: e.stream_shuffle(sh_s[:, :], raw_s[:, :],
                                                     PAIRMASK),
                          wait=[(sc, sqN), (dC0, dC0.n)])
                        v(lambda e: e.drain())
                        v(lambda e, ps=ps, tb=tabc, c=c: e.tensor_tensor(
                            t1_s[:, :], ps[:, :],
                            tb[:, c * 512:(c + 1) * 512], ALU.mult), inc=vc)
                        bset(bk, vc)
                        v(lambda e, tb=tabs, c=c: e.tensor_tensor(
                            t2_s[:, :], sh_s[:, :],
                            tb[:, c * 512:(c + 1) * 512], ALU.mult))
                        ro = rope_o[et % 2]
                        key = et % 2
                        wsp = ([ro_spill[key]] if key in ro_spill else [])
                        v(lambda e, ro=ro: e.tensor_tensor(
                            ro[:, :], t1_s[:, :], t2_s[:, :], ALU.add),
                          wait=wsp, inc=vc)
                        ropeN = vc.n
                        ro_spill["rope_vc"] = ropeN
                        for half in range(2):
                            h = 2 * et + half
                            t_mm(psE[0:1, 0:512],
                                 ones_bf[half * 64:(half + 1) * 64, 0:1],
                                 sq_s[half * 64:(half + 1) * 64, :],
                                 start=True, stop=True,
                                 wait=bwait("E") + [(sc, sqN)], inc=tc)
                            nmN = tc.n
                            v(lambda e: e.tensor_reduce(
                                tcols[0:1, 4:5], psE[0:1, 0:512], AX.X,
                                ALU.max), wait=[(tc, nmN)], inc=vc)
                            bset("E", vc)
                            v(lambda e: e.drain())
                            v(lambda e, m2=m2, h=h: e.tensor_tensor(
                                m2[0:1, h:h + 1], m2[0:1, h:h + 1],
                                tcols[0:1, 4:5], ALU.max), inc=vc)
                        dma("sync", scrap[b, et, :, c * 512:(c + 1) * 512],
                            ro[:, :], wait=[(vc, ropeN)], inc=dSpro[key])
                        ro_spill[key] = (dSpro[key], dSpro[key].n)
                for i in range(4):
                    kt = c * 4 + i
                    vst = vstage[i % 2]
                    for n2 in range(2):
                        ps, bk = (psC, "C") if n2 == 0 else (psD, "D")
                        for dt in range(8):
                            t_mm(ps[:, :],
                                 xnT[:, dt * 512 + i * 128:
                                     dt * 512 + (i + 1) * 128],
                                 wqkv_b[:, dt * 3 * D + 2 * D + n2 * 512:
                                        dt * 3 * D + 2 * D + (n2 + 1) * 512],
                                 start=(dt == 0), stop=(dt == 7),
                                 wait=bwait(bk) if dt == 0 else (),
                                 inc=tc if dt == 7 else None)
                        mmN = tc.n
                        wsp = ([vst_spill[i % 2]]
                               if (i % 2) in vst_spill and n2 == 0 else [])
                        v(lambda e, ps=ps, vst=vst, n2=n2: e.tensor_copy(
                            vst[:, :].rearrange("p (h q) -> p h q", h=16)
                            [:, n2 * 8:(n2 + 1) * 8, 0:64],
                            ps[:, :].rearrange("p (h q) -> p h q", h=8)),
                          wait=wsp + [(tc, mmN)], inc=vc)
                        bset(bk, vc)
                    v(lambda e, vst=vst: e.memset(
                        vst[:, :].rearrange("p (h q) -> p h q", h=16)
                        [:, :, 64:65], 1.0), inc=vc)
                    vN = vc.n
                    dma("sync", vscr[b, kt, :, :], vst[:, :],
                        wait=[(vc, vN)], inc=dSpv[i % 2])
                    vst_spill[i % 2] = (dSpv[i % 2], dSpv[i % 2].n)
                if early is not None:
                    early["pump"]()
                if pidx + 1 < len(pseq):
                    proj_stats(*pseq[pidx + 1])

            if c != 3 and "xnonly" not in phases:
                continue
            v(lambda e: e.tensor_tensor(tcols[0:1, 16:32], m2q[0:1, :],
                                        m2k[0:1, :], ALU.mult))
            v(lambda e: e.drain(), inc=vc)
            mvN = vc.n
            s(lambda e: e.activation(tcols[0:1, 16:32], tcols[0:1, 16:32],
                                     AF.Sqrt), wait=[(vc, mvN)], inc=sc)
            msN = sc.n
            v(lambda e: e.tensor_scalar(tcols[0:1, 32:48], tcols[0:1, 16:32],
                                        -0.125, None, ALU.mult),
              wait=[(sc, msN)])
            v(lambda e: e.drain())
            v(lambda e: e.tensor_copy(raw_s[0:1, 0:16], tcols[0:1, 32:48]))
            v(lambda e: e.drain(), inc=vc)
            mrN = vc.n
            t_mm(psE[0:128, 0:16], orow_bf[0:1, :], raw_s[0:1, 0:16],
                 start=True, stop=True, wait=bwait("E") + [(vc, mrN)],
                 inc=tc)
            mtN = tc.n
            v(lambda e, b=b: e.tensor_copy(mbias[:, b * 16:(b + 1) * 16],
                                           psE[0:128, 0:16]),
              wait=[(tc, mtN)], inc=vc)
            bset("E", vc)
            mbias_vc[b] = vc.n
            if b == 0:
                v(lambda e: e.memset(m2q[:, :], 0.0))
                v(lambda e: e.memset(m2k[:, :], 0.0), inc=vc)

        proj_tc, proj_vc, proj_sc = tc.n, vc.n, sc.n
        ar.release()

        if "l0proj" in phases:
            dbg = nc.dram_tensor("dbg", [2, B, 8, 128, T], BF16,
                                 kind="ExternalOutput").ap()
            dbgv = nc.dram_tensor("dbgv", [B, 16, 128, 1040], BF16,
                                  kind="ExternalOutput").ap()
            spw = [(dSpro[0], dSpro[0].n), (dSpro[1], dSpro[1].n),
                   (dSpv[0], dSpv[0].n), (dSpv[1], dSpv[1].n)]
            for b in range(B):
                for et in range(8):
                    dma("sync", dbg[0, b, et, :, :], qscr[b, et, :, :],
                        wait=spw, inc=dDBG)
                    dma("sync", dbg[1, b, et, :, :], kscr[b, et, :, :],
                        inc=dDBG)
                for kt in range(16):
                    dma("sync", dbgv[b, kt, :, :], vscr[b, kt, :, :],
                        inc=dDBG)

        kwin = [ar.alloc([128, T], BF16, f"kwin{i}") for i in range(2)]
        qwin = [ar.alloc([128, 512], BF16, f"qwin{i}") for i in range(2)]
        vwin = [ar.alloc([128, 1040], BF16, f"vwin{i}") for i in range(16)]
        ptile = [ar.alloc([128, 512], BF16, f"ptile{i}") for i in range(2)]
        rrow = ar.alloc([1, 512], BF16, "rrow")
        bcb = ar.alloc([64, 512], BF16, "bcb")
        attnT = ar.alloc([128, 8 * 512], BF16, "attnT")
        alias_w = [(tc, proj_tc), (vc, proj_vc), (sc, proj_sc)]

        dKW = [p.ctr("dKW0"), p.ctr("dKW1")]
        dQW = [p.ctr("dQW0"), p.ctr("dQW1")]
        dVW = p.ctr("dVW")
        spfin = [(dSpro[0], dSpro[0].n), (dSpro[1], dSpro[1].n),
                 (dSpv[0], dSpv[0].n), (dSpv[1], dSpv[1].n)]
        prev_pv_tc = 0
        for b in range(B) if attn_sub else []:
            kq_used = {}
            prev_y_tc = 0
            for kt in range(16):
                dma("sync", vwin[kt][:, :], vscr[b, kt, :, :],
                    wait=alias_w + spfin + [(tc, prev_pv_tc)], inc=dVW)
            vw_fin = dVW.n
            for qc in range(4):
                ext = (qc + 1) * 512
                for et in range(8):
                    kbuf = kwin[et % 2]
                    qbuf = qwin[et % 2]
                    wprev = ([(tc, kq_used[et % 2])]
                             if (et % 2) in kq_used else [])
                    dma("sync", kbuf[:, 0:ext], kscr[b, et, :, 0:ext],
                        wait=alias_w + wprev + spfin, inc=dKW[et % 2])
                    dma("sync", qbuf[:, :],
                        qscr[b, et, :, qc * 512:(qc + 1) * 512],
                        wait=wprev, inc=dQW[et % 2])
                    kq_ready = [(dKW[et % 2], dKW[et % 2].n),
                                (dQW[et % 2], dQW[et % 2].n)]
                    if dbg3_ap is not None and b == 0 and qc == 0 \
                            and et == 0:
                        dma("gpsimd", dbg3_ap[0, :, 0:32], mbias[:, :],
                            wait=[(vc, mbias_vc[B - 1])], inc=dDBG2)
                        dma("gpsimd", dbg3_ap[2, :, 0:2048], kbuf[:, :],
                            wait=kq_ready, inc=dDBG2)
                        dma("gpsimd", dbg3_ap[3, :, 0:512], qbuf[:, :],
                            wait=kq_ready, inc=dDBG2)
                    for half in range(2):
                        h = 2 * et + half
                        pv = psA[:, half * 512:(half + 1) * 512]
                        nkt = qc * 4 + 4
                        for kt in range(nkt):
                            sps, sbk = ((psC, "C") if kt % 2 == 0
                                        else (psD, "D"))
                            t_mm(sps[:, :],
                                 kbuf[half * 64:(half + 1) * 64,
                                      kt * 128:(kt + 1) * 128],
                                 qbuf[half * 64:(half + 1) * 64, :],
                                 start=True, stop=True,
                                 wait=bwait(sbk) + kq_ready, inc=tc)
                            scN = tc.n
                            pt = ptile[kt % 2]
                            s(lambda e, sps=sps, pt=pt, b=b, h=h:
                              e.activation(pt[:, :], sps[:, :], AF.Exp,
                                           bias=mbias[:,
                                                      b * 16 + h:
                                                      b * 16 + h + 1]),
                              wait=[(tc, scN), (vc, mbias_vc[b])], inc=sc)
                            bset(sbk, sc)
                            pctr, pn = sc, sc.n
                            if kt >= qc * 4:
                                off = (kt - qc * 4) * 128
                                st0 = 512 - off
                                v(lambda e, pt=pt, st0=st0: e.tensor_tensor(
                                    pt[:, :], pt[:, :],
                                    bigm_t[:, st0:st0 + 512], ALU.mult),
                                  wait=[(sc, pn)], inc=vc)
                                pctr, pn = vc, vc.n
                            t_mm(pv[0:65, :],
                                 vwin[kt][:, h * 65:(h + 1) * 65], pt[:, :],
                                 start=(kt == 0), stop=(kt == nkt - 1),
                                 wait=(bwait("A") if kt == 0 else []) +
                                 [(pctr, pn), (dVW, vw_fin)],
                                 inc=tc if kt == nkt - 1 else None)
                        pvN = tc.n
                        v(lambda e, pv=pv: e.reciprocal(rrow[0:1, :],
                                                        pv[64:65, :]),
                          wait=[(tc, pvN)])
                        v(lambda e: e.drain(), inc=vc)
                        rN = vc.n
                        t_mm(psB[0:64, half * 512:(half + 1) * 512],
                             orow_bf[0:1, 0:64], rrow[0:1, :],
                             start=True, stop=True,
                             wait=bwait("B") + [(vc, rN)], inc=tc)
                        bcN = tc.n
                        s(lambda e, half=half: e.copy(
                            bcb[:, :],
                            psB[0:64, half * 512:(half + 1) * 512]),
                          wait=[(tc, bcN)], inc=sc)
                        bset("B", sc)
                        bN = sc.n
                        v(lambda e, pv=pv, et=et, half=half:
                          e.tensor_tensor(
                              attnT[half * 64:(half + 1) * 64,
                                    et * 512:(et + 1) * 512],
                              pv[0:64, :], bcb[:, :], ALU.mult),
                          wait=[(sc, bN), (tc, prev_y_tc)], inc=vc)
                        bset("A", vc)
                    kq_used[et % 2] = tc.n
                attn_vc = vc.n
                if dbg3_ap is not None and b == 0 and qc == 0:
                    dma("gpsimd", dbg3_ap[1, :, 0:4096], attnT[:, :],
                        wait=[(vc, attn_vc)], inc=dDBG2)
                for i in range(4):
                    g = b * 16 + qc * 4 + i
                    for n2 in range(2):
                        for kt in range(8):
                            t_mm(psB[:, n2 * 512:(n2 + 1) * 512],
                                 attnT[:, kt * 512 + i * 128:
                                       kt * 512 + (i + 1) * 128],
                                 wproj_b[:, kt * D + n2 * 512:
                                         kt * D + (n2 + 1) * 512],
                                 start=(kt == 0), stop=(kt == 7),
                                 wait=(bwait("B") +
                                       [(vc, attn_vc), (dWP, wp_fin)])
                                 if kt == 0 else (),
                                 inc=tc if kt == 7 else None)
                    yN = tc.n
                    ln_stats(psB, D, 4, wait=[(tc, yN)], lead=False)
                    rn = rstd_from_var(5, 6)
                    v(lambda e: e.tensor_scalar(
                        fscr[:, :], psB[:, :], tcols[:, 4:5], tcols[:, 6:7],
                        ALU.subtract, ALU.mult), wait=[(sc, rn)], inc=vc)
                    bset("B", vc)
                    v(lambda e, g=g: e.tensor_tensor(
                        res_t[:, g * D:(g + 1) * D],
                        res_t[:, g * D:(g + 1) * D], fscr[:, :], ALU.add),
                      inc=vc)
                prev_y_tc = tc.n
                if early is not None:
                    early["pump"]()
            prev_pv_tc = tc.n
        if early is not None:
            early["pump"]()
        ar.release()

    def emit_ffn(layer, early=None, region=None):
        if early is not None:
            region = early["addr"]
        ar.mark()
        w2res = ar.alloc([128, 32 * 1024], BF16, "w2res")
        lnwc = ar.alloc([128, 32], F32, "lnwc")
        lnbc_ = ar.alloc([128, 32], F32, "lnbcf")
        lnbcb = ar.alloc([128, 32], BF16, "lnbcb")
        rgb = ar.alloc([128, 2], F32, "rgb")
        acc_c = ar.alloc([128, 2], F32, "accc")
        trowA = ar.alloc([1, 512], F32, "trowA")
        trowB = ar.alloc([1, 512], F32, "trowB")
        trow_b = ar.alloc([1, 2048], BF16, "trowb")

        dBR = [p.ctr(f"dBR{layer}{j}") for j in range(4)]
        dB2 = [p.ctr(f"dB2{layer}a"), p.ctr(f"dB2{layer}b")]
        dW1S = p.ctr(f"dW1S{layer}")
        dW1 = [p.ctr(f"dW1{layer}a"), p.ctr(f"dW1{layer}b")]
        dLNW = p.ctr("dLNW") if layer == 0 else None
        dLNB = p.ctr("dLNB") if layer == 0 else None
        w1_in = w_fc if layer == 0 else w_up
        w2_in = w_pr0 if layer == 0 else w_dn
        w1scr = w1t_scr[layer]
        eps_g = 1e-6 if layer == 0 else 0.0

        ar.mark()
        if region is not None:
            braw = [nc.alloc_sbuf_tensor_at(f"brawV{layer}{i}",
                                            [128, 1024], F32,
                                            offset=region[i])
                    for i in range(4)]
            braw2 = [nc.alloc_sbuf_tensor_at(f"braw2V{layer}{i}",
                                             [128, 2048], F32,
                                             offset=region[2 * i])
                     for i in range(2)]
        else:
            braw = [ar.alloc([128, 1024], F32, f"braw{layer}{i}")
                    for i in range(2)]
            braw2 = [ar.alloc([128, 2048], F32, f"braw2{layer}{i}")
                     for i in range(2)]
        if early is not None:
            acc_c = early["acc"]
        tful = ar.alloc([128, 2048], F32, "tful")
        qful = ar.alloc([128, 2048], BF16, "qful")

        phase_guard = [(tc, tc.n), (vc, vc.n), (sc, sc.n)]
        if early is None:
            v(lambda e: e.memset(acc_c[:, :], 0.0), inc=vc)
            for r in range(32):
                br = braw[r % 2]
                wprev = [(vc, vc.n)] if r >= 2 else phase_guard
                dma("gpsimd", br[:, :], w1_in[r * 128:(r + 1) * 128, :],
                    wait=wprev, inc=dBR[r % 2])
                ldN = (dBR[r % 2], dBR[r % 2].n)
                for k in range(2):
                    v(lambda e, br=br, k=k: e.tensor_reduce(
                        tcols[:, 8 + k:9 + k], br[:, k * 512:(k + 1) * 512],
                        AX.X, ALU.add, apply_absolute_value=True),
                      wait=[ldN] if k == 0 else ())
                v(lambda e: e.drain())
                v(lambda e: e.tensor_tensor(tcols[:, 8:9], tcols[:, 8:9],
                                            tcols[:, 9:10], ALU.add))
                v(lambda e: e.drain())
                v(lambda e: e.tensor_tensor(acc_c[:, 0:1], acc_c[:, 0:1],
                                            tcols[:, 8:9], ALU.add), inc=vc)
            for r in range(16):
                br2 = braw2[r % 2]
                wprev = [(vc, vc.n)] if r >= 2 else \
                    (phase_guard + [(vc, vc.n)])
                dma("gpsimd", br2[:, :],
                    w2_in[(r // 2) * 128:(r // 2 + 1) * 128,
                          (r % 2) * 2048:(r % 2 + 1) * 2048], wait=wprev,
                    inc=dB2[r % 2])
                ldN = (dB2[r % 2], dB2[r % 2].n)
                for k in range(4):
                    v(lambda e, br2=br2, k=k: e.tensor_reduce(
                        tcols[:, 8 + (k % 2):9 + (k % 2)],
                        br2[:, k * 512:(k + 1) * 512],
                        AX.X, ALU.add, apply_absolute_value=True),
                      wait=[ldN] if k == 0 else ())
                    v(lambda e: e.drain())
                    v(lambda e, k=k: e.tensor_tensor(
                        acc_c[:, 1:2], acc_c[:, 1:2],
                        tcols[:, 8 + (k % 2):9 + (k % 2)], ALU.add),
                      inc=vc if k == 3 else None)
            g_vc = vc.n
        else:
            g_vc = early["cons_vc"]
        for wi in range(2):
            t_mm(psE[0:1, 0:1], ones_f32[:, 0:1], acc_c[:, wi:wi + 1],
                 start=True, stop=True, wait=bwait("E") + [(vc, g_vc)],
                 inc=tc)
            gN = tc.n
            v(lambda e, wi=wi: e.tensor_scalar(
                tcols[0:1, 16 + wi:17 + wi], psE[0:1, 0:1],
                float(1.0 / (FF * D)), float(eps_g), ALU.mult, ALU.add),
              wait=[(tc, gN)], inc=vc)
            bset("E", vc)
        v(lambda e: e.drain())
        v(lambda e: e.reciprocal(tcols[0:1, 18:20], tcols[0:1, 16:18]))
        v(lambda e: e.drain(), inc=vc)
        rg_vc = vc.n
        for wi in range(2):
            t_mm(psE[0:128, wi:wi + 1], orow_f32[0:1, :],
                 tcols[0:1, 18 + wi:19 + wi], start=True, stop=True,
                 wait=bwait("E") + [(vc, rg_vc)],
                 inc=tc if wi == 1 else None)
        rgN = tc.n
        v(lambda e: e.tensor_copy(rgb[:, 0:2], psE[0:128, 0:2]),
          wait=[(tc, rgN)], inc=vc)
        bset("E", vc)
        rgb_vc = vc.n
        if layer == 0:
            dma("sync", lnwc[:, :], ln_w.rearrange("(a p2) -> p2 a", p2=128),
                inc=dLNW, slow=True)
            dma("gpsimd", lnbc_[:, :], ln_b.rearrange("(a p2) -> p2 a", p2=128),
                inc=dLNB, slow=True)
            v(lambda e: e.drain(), wait=[(dLNB, dLNB.n), (dLNW, dLNW.n)])
            v(lambda e: e.reciprocal(tcols[:, 32:64], lnwc[:, :]))
            v(lambda e: e.drain())
            v(lambda e: e.tensor_tensor(lnbcb[:, :], lnbc_[:, :],
                                        tcols[:, 32:64], ALU.mult), inc=vc)

        prev_sp = {}
        last_q_vc = 0
        nd = len(braw)
        act_hist = []
        for r in range(32):
            br = braw[r % nd]
            wprev = [(sc, act_hist[r - nd])] if r >= nd else [(vc, rgb_vc)]
            dma("gpsimd", br[:, :], w1_in[r * 128:(r + 1) * 128, :],
                wait=wprev, inc=dBR[r % nd])
            ldN = (dBR[r % nd], dBR[r % nd].n)
            s(lambda e, br=br: e.activation(
                tful[:, 0:1024], br[:, :], AF.Identity, bias=magic_t[:, 0:1],
                scale=rgb[:, 0:1]),
              wait=[ldN, (dC0, dC0.n), (vc, last_q_vc)], inc=sc)
            sN = sc.n
            act_hist.append(sN)
            v(lambda e: e.tensor_scalar(tful[:, 0:1024], tful[:, 0:1024],
                                        float(MAGIC), 1.0, ALU.subtract,
                                        ALU.min), wait=[(sc, sN)])
            wq = [prev_sp[r % 2], (dW1S, dW1S.n)] if (r % 2) in prev_sp \
                else []
            v(lambda e: e.tensor_scalar(qful[:, 0:1024], tful[:, 0:1024],
                                        -1.0, None, ALU.max),
              wait=wq, inc=vc)
            qN = vc.n
            last_q_vc = qN
            trN = tr8(qful, wait=[(vc, qN)])
            qv4 = qful[:, 1024:2048].rearrange(
                "p (a two c2) -> p a two c2", a=4, two=2)
            v(lambda e: e.nop(), wait=[(dW1S, dW1S.n)])
            evac_pair(qv4[:, :, 0, :], qv4[:, :, 1, :], trN)
            evN = vc.n
            for dt in range(8):
                dma("sync", w1scr[dt * 128:(dt + 1) * 128,
                                  r * 128:(r + 1) * 128],
                    qful[:, 1024 + dt * 128:1024 + (dt + 1) * 128],
                    wait=[(vc, evN)], inc=dW1S)
            prev_sp[r % 2] = (dW1S, dW1S.n)

        act2_hist = []
        for r in range(16):
            br2 = braw2[r % 2]
            dtile = r // 2
            fhalf = r % 2
            wprev = [(sc, act2_hist[r - 2])] if r >= 2 else [(sc, sc.n)]
            dma("gpsimd", br2[:, :],
                w2_in[dtile * 128:(dtile + 1) * 128,
                      fhalf * 2048:(fhalf + 1) * 2048], wait=wprev,
                inc=dB2[r % 2])
            ldN = (dB2[r % 2], dB2[r % 2].n)
            s(lambda e, br2=br2: e.activation(
                tful[:, :], br2[:, :], AF.Identity, bias=magic_t[:, 0:1],
                scale=rgb[:, 1:2]), wait=[ldN, (vc, last_q_vc)], inc=sc)
            sN = sc.n
            act2_hist.append(sN)
            v(lambda e: e.tensor_scalar(tful[:, :], tful[:, :], float(MAGIC),
                                        1.0, ALU.subtract, ALU.min),
              wait=[(sc, sN)])
            v(lambda e: e.tensor_scalar(qful[:, :], tful[:, :], -1.0, None,
                                        ALU.max), wait=[(dW1S, dW1S.n)],
              inc=vc)
            qN = vc.n
            last_q_vc = qN
            for fq in range(2):
                trN = tr8(qful[:, fq * 1024:(fq + 1) * 1024],
                          wait=[(vc, qN)])
                f0 = fhalf * 16 + fq * 8
                if layer == 0:
                    for fk in range(8):
                        bank = ptA if fk % 2 == 0 else ptEb
                        v(lambda e, fk=fk, f0=f0, dtile=dtile, bank=bank:
                          e.tensor_scalar(
                              w2res[:, (f0 + fk) * 1024 + dtile * 128:
                                    (f0 + fk) * 1024 + (dtile + 1) * 128],
                              bank[:, fk * 128:(fk + 1) * 128],
                              lnwc[:, f0 + fk:f0 + fk + 1], None, ALU.mult),
                          wait=([(tc, trN), (dLNW, 16)]
                                if fk == 0 else ()),
                          inc=vc if fk == 7 else None)
                    bset("T", vc)
                    bset("E", vc)
                else:
                    wv = w2res[:, :].rearrange(
                        "p (a two c2) -> p a two c2", a=16, two=2)
                    evac_pair(
                        wv[:, f0 // 2:f0 // 2 + 4, 0,
                           dtile * 128:(dtile + 1) * 128],
                        wv[:, f0 // 2:f0 // 2 + 4, 1,
                           dtile * 128:(dtile + 1) * 128], trN)
        w2_vc = vc.n
        if dbg4_ap is not None and layer == 0:
            for r8 in range(8):
                dma("sync", dbg4_ap[r8 * 128:(r8 + 1) * 128, :],
                    w1scr[r8 * 128:(r8 + 1) * 128, :],
                    wait=[(dW1S, dW1S.n)] if r8 == 0 else (), inc=dDBG)
        ar.release()

        w1sl = [ar.alloc([128, 8 * 512], BF16, f"w1sl{i}") for i in range(2)]
        gT = ar.alloc([128, 32 * 512], BF16, "gT")
        xn_st = ar.alloc([128, D], BF16, "xnstf")
        if region is not None:
            a0 = region[0]
            fscr2 = nc.alloc_sbuf_tensor_at(f"fscr2E{layer}", [128, 1024],
                                            F32, offset=a0)
            xnTf = nc.alloc_sbuf_tensor_at(f"xnTfE{layer}", [128, 8 * 512],
                                           BF16, offset=a0 + 4096)
            ybuf = nc.alloc_sbuf_tensor_at(f"ybufE{layer}", [128, 1024],
                                           BF16, offset=a0 + 12288)
            ubc = nc.alloc_sbuf_tensor_at(f"ubcE{layer}", [128, 1024],
                                          BF16, offset=a0 + 14336)
            bbc = ar.alloc([128, 1024], BF16, f"bbc{layer}")
        else:
            xnTf = ar.alloc([128, 8 * 512], BF16, f"xnTf{layer}")
            ybuf = ar.alloc([128, 1024], BF16, f"ybuf{layer}")
            fscr2 = ar.alloc([128, 1024], F32, f"fscr2{layer}")
            ubc = ar.alloc([128, 1024], BF16, f"ubc{layer}")
            bbc = ar.alloc([128, 1024], BF16, f"bbc{layer}")
        sqst = xn_st

        if layer == 0:
            for row, lh in ((0, None), (1, None)):
                for n2 in range(2):
                    for kt in range(32):
                        lhsT = (ones_bf[:, 0:1] if row == 0
                                else lnbcb[:, kt:kt + 1])
                        t_mm(psE[0:1, 0:512], lhsT,
                             w2res[:, kt * 1024 + n2 * 512:
                                   kt * 1024 + (n2 + 1) * 512],
                             start=(kt == 0), stop=(kt == 31),
                             wait=(bwait("E") + [(vc, w2_vc)])
                             if kt == 0 else (),
                             inc=tc if kt == 31 else None)
                    uN = tc.n
                    v(lambda e, row=row, n2=n2: e.tensor_copy(
                        trow_b[0:1, row * 1024 + n2 * 512:
                               row * 1024 + (n2 + 1) * 512],
                        psE[0:1, 0:512]), wait=[(tc, uN)])
                    v(lambda e: e.drain(), inc=vc)
                    bset("E", vc)
            ub_vc = vc.n
            for row, tgt in ((0, ubc), (1, bbc)):
                for n2 in range(2):
                    t_mm(psB[:, n2 * 512:(n2 + 1) * 512], orow_bf[0:1, :],
                         trow_b[0:1, row * 1024 + n2 * 512:
                                row * 1024 + (n2 + 1) * 512],
                         start=True, stop=True,
                         wait=bwait("B") + [(vc, ub_vc)],
                         inc=tc if n2 == 1 else None)
                ubN = tc.n
                v(lambda e, tgt=tgt: e.tensor_copy(tgt[:, :], psB[:, :]),
                  wait=[(tc, ubN)], inc=vc)
                bset("B", vc)

        for g in range(NT):
            ln_stats(res_t[:, g * D:(g + 1) * D], D, 0,
                     wait=[(dX, x_fin)] if g == 0 else ())
            rn = rstd_from_var(1, 2)
            if layer == 1:
                v(lambda e: e.tensor_scalar(tcols[:, 3:4], tcols[:, 1:2],
                                            1e-30, None, ALU.add))
                v(lambda e: e.drain(), inc=vc)
                vn2 = vc.n
                s(lambda e: e.activation(tcols[:, 5:6], tcols[:, 3:4],
                                         AF.Sqrt), wait=[(vc, vn2)], inc=sc)
                rn = sc.n
                v(lambda e: e.tensor_tensor(tcols[:, 3:4], tcols[:, 5:6],
                                            tcols[:, 2:3], ALU.mult),
                  wait=[(sc, rn)])
                v(lambda e: e.drain())
                v(lambda e: e.tensor_scalar(tcols[:, 3:4], tcols[:, 3:4],
                                            1e-8, None, ALU.add))
                v(lambda e: e.drain())
                v(lambda e: e.reciprocal(tcols[:, 4:5], tcols[:, 3:4]))
                v(lambda e: e.drain())
                v(lambda e: e.tensor_tensor(tcols[:, 2:3], tcols[:, 2:3],
                                            tcols[:, 4:5], ALU.mult))
                v(lambda e: e.drain())
                rn = 0
            v(lambda e, g=g: e.tensor_copy(precols[:, g * 2:g * 2 + 1],
                                           tcols[:, 0:1]),
              wait=[(sc, rn)] if rn else ())
            v(lambda e, g=g: e.tensor_copy(precols[:, g * 2 + 1:g * 2 + 2],
                                           tcols[:, 2:3]), inc=vc)
        pre_vc = vc.n

        out_dma = 0
        w1_used = {}
        for c in range(8):
            for i in range(4):
                g = c * 4 + i
                v(lambda e, g=g: e.tensor_scalar(
                    xn_st[:, :], res_t[:, g * D:(g + 1) * D],
                    precols[:, g * 2:g * 2 + 1],
                    precols[:, g * 2 + 1:g * 2 + 2],
                    ALU.subtract, ALU.mult), wait=[(vc, pre_vc)], inc=vc)
                xnN = vc.n
                trN = tr8(xn_st, wait=[(vc, xnN)])
                xv4 = xnTf[:, :].rearrange(
                    "p (a two c2) -> p a two c2", a=4, two=2)
                evac_pair(xv4[:, :, 0, i * 128:(i + 1) * 128],
                          xv4[:, :, 1, i * 128:(i + 1) * 128], trN)
            xnT_vc = vc.n

            for fg in range(8):
                wsl = w1sl[fg % 2]
                wprev = ([(tc, w1_used[fg % 2])]
                         if (fg % 2) in w1_used else [(vc, w2_vc)])
                for kt in range(8):
                    dma("sync", wsl[:, kt * 512:(kt + 1) * 512],
                        w1scr[kt * 128:(kt + 1) * 128,
                              fg * 512:(fg + 1) * 512],
                        wait=(wprev + [(dW1S, dW1S.n)])
                        if kt == 0 else (), inc=dW1[fg % 2])
                w1N = (dW1[fg % 2], dW1[fg % 2].n)
                for fi in range(4):
                    ft = fg * 4 + fi
                    ps, bk = (psC, "C") if ft % 2 == 0 else (psD, "D")
                    for kt in range(8):
                        t_mm(ps[:, :],
                             wsl[:, kt * 512 + fi * 128:
                                 kt * 512 + (fi + 1) * 128],
                             xnTf[:, kt * 512:(kt + 1) * 512],
                             start=(kt == 0), stop=(kt == 7),
                             wait=(bwait(bk) + [(vc, xnT_vc), w1N,
                                                (dC0, dC0.n)])
                             if kt == 0 else (),
                             inc=tc if kt == 7 else None)
                    mmN = tc.n
                    s(lambda e, ps=ps, ft=ft: e.activation(
                        gT[:, ft * 512:(ft + 1) * 512], ps[:, :], AF.Gelu),
                      wait=[(tc, mmN)], inc=sc)
                    bset(bk, sc)
                    geN = sc.n
                    v(lambda e, ft=ft: e.tensor_tensor(
                        sqst[:, 0:512], gT[:, ft * 512:(ft + 1) * 512],
                        gT[:, ft * 512:(ft + 1) * 512], ALU.mult),
                      wait=[(sc, geN)], inc=vc)
                    sqN = vc.n
                    if layer == 0:
                        t_mm(psB[0:1, 0:512], ones_bf[:, 0:1],
                             gT[:, ft * 512:(ft + 1) * 512],
                             start=(ft == 0), stop=(ft == 31),
                             wait=(bwait("B") + [(sc, geN)]) if ft == 0
                             else [(sc, geN)])
                    t_mm(psB[0:1, 512:1024] if layer == 0
                         else psB[0:1, 0:512],
                         ones_bf[:, 0:1], sqst[:, 0:512],
                         start=(ft == 0), stop=(ft == 31),
                         wait=(bwait("B") + [(vc, sqN)]) if ft == 0
                         else [(vc, sqN)],
                         inc=tc if ft == 31 else None)
                w1_used[fg % 2] = tc.n
            stN = tc.n
            nr = 2 if layer == 0 else 1
            v(lambda e: e.tensor_copy(trowA[0:1, 0:512], psB[0:1, 0:512]),
              wait=[(tc, stN)])
            if layer == 0:
                v(lambda e: e.tensor_copy(trowB[0:1, 0:512],
                                          psB[0:1, 512:1024]))
            v(lambda e: e.drain(), inc=vc)
            bset("B", vc)
            strN = vc.n
            if dbg2_ap is not None and c == 0 and layer == layer:
                dma("sync", dbg2_ap[0, 0:1, 0:512], trowA[0:1, 0:512],
                    wait=[(vc, strN)], inc=dDBG)
                if layer == 0:
                    dma("sync", dbg2_ap[0, 0:1, 512:1024], trowB[0:1, 0:512],
                        inc=dDBG)
                dma("gpsimd", dbg2_ap[2, :, 0:4096],
                    gT[:, 0:4096], wait=[(vc, strN)], inc=dDBG2)
                dma("gpsimd", dbg2_ap[3, :, 0:4096],
                    xnTf[:, 0:4096], wait=[(vc, strN)], inc=dDBG2)
            ptAf = ptA[:, :].bitcast(F32)
            for i in range(4):
                rows = (trowA,) if layer == 1 else (trowA, trowB)
                for ri, rowt in enumerate(rows):
                    last_tr = (i == 3 and ri == len(rows) - 1)
                    bank = psE if ri == 0 else ptAf
                    p.add("tensor",
                          lambda e, i=i, bank=bank, rowt=rowt: e.transpose(
                              bank[0:128, 16 + i:17 + i],
                              rowt[0:1, i * 128:(i + 1) * 128],
                              id4f_t[0:1, 0:1]),
                          wait=(bwait("E") + bwait("T") + [(vc, strN)])
                          if (i == 0 and ri == 0) else (),
                          inc=tc if last_tr else None)
            scN2 = tc.n
            v(lambda e: e.tensor_copy(tcols[:, 24:28], psE[0:128, 16:20]),
              wait=[(tc, scN2)], inc=vc)
            if layer == 0:
                v(lambda e: e.tensor_copy(tcols[:, 28:32],
                                          ptAf[0:128, 16:20]), inc=vc)
            bset("E", vc)
            bset("T", vc)
            v(lambda e: e.drain())
            if layer == 0:
                v(lambda e: e.tensor_scalar(tcols[:, 48:52], tcols[:, 24:28],
                                            float(1.0 / FF), None, ALU.mult))
                v(lambda e: e.tensor_scalar(tcols[:, 52:56], tcols[:, 28:32],
                                            float(1.0 / FF), float(LN_EPS),
                                            ALU.mult, ALU.add))
                v(lambda e: e.tensor_tensor(tcols[:, 56:60], tcols[:, 48:52],
                                            tcols[:, 48:52], ALU.mult))
                v(lambda e: e.drain())
                v(lambda e: e.tensor_tensor(tcols[:, 44:48], tcols[:, 52:56],
                                            tcols[:, 56:60], ALU.subtract))
                v(lambda e: e.drain())
                v(lambda e: e.reciprocal(tcols[:, 56:60], tcols[:, 44:48]))
                v(lambda e: e.drain(), inc=vc)
                vb = vc.n
                s(lambda e: e.activation(tcols[:, 52:56], tcols[:, 56:60],
                                         AF.Sqrt), wait=[(vc, vb)], inc=sc)
                rstd_sc = sc.n
                v(lambda e: e.tensor_tensor(tcols[:, 56:60], tcols[:, 48:52],
                                            tcols[:, 52:56], ALU.mult),
                  wait=[(sc, rstd_sc)], inc=vc)
            else:
                v(lambda e: e.tensor_scalar(tcols[:, 48:52], tcols[:, 24:28],
                                            1e-30, None, ALU.add))
                v(lambda e: e.drain(), inc=vc)
                vb = vc.n
                s(lambda e: e.activation(tcols[:, 52:56], tcols[:, 48:52],
                                         AF.Sqrt), wait=[(vc, vb)], inc=sc)
                rstd_sc = sc.n
                _rstd_ln_sc[0] = rstd_sc
                v(lambda e: e.tensor_scalar(tcols[:, 48:52], tcols[:, 52:56],
                                            float(1.0 / 64.0), 1e-8,
                                            ALU.mult, ALU.add),
                  wait=[(sc, rstd_sc)])
                v(lambda e: e.drain())
                v(lambda e: e.reciprocal(tcols[:, 52:56], tcols[:, 48:52]))
                v(lambda e: e.drain(), inc=vc)
            sc_vc = vc.n
            if dbg2_ap is not None and c == 0:
                dma("sync", dbg2_ap[0, :, 1024:1064], tcols[:, 24:64],
                    wait=[(vc, sc_vc)], inc=dDBG)
            for i in range(4):
                g = c * 4 + i
                for n2 in range(2):
                    for kt in range(32):
                        t_mm(psA[:, n2 * 512:(n2 + 1) * 512],
                             gT[:, kt * 512 + i * 128:
                                kt * 512 + (i + 1) * 128],
                             w2res[:, kt * 1024 + n2 * 512:
                                   kt * 1024 + (n2 + 1) * 512],
                             start=(kt == 0), stop=(kt == 31),
                             wait=(bwait("A") + [(vc, sc_vc)])
                             if kt == 0 else (),
                             inc=tc if kt == 31 else None)
                mmN2 = tc.n
                if layer == 0:
                    rs_c, c2_c = 52 + i, 56 + i
                    wout = [(dOUT, out_dma)] if out_dma else []
                    v(lambda e, c2_c=c2_c: e.tensor_scalar(
                        tcols[:, 14:15], tcols[:, c2_c:c2_c + 1], -1.0,
                        None, ALU.mult), wait=wout + [(tc, mmN2)])
                    v(lambda e: e.drain())
                    v(lambda e: e.scalar_tensor_tensor(
                        fscr2[:, :], ubc[:, :], tcols[:, 14:15], bbc[:, :],
                        ALU.mult, ALU.add))
                    v(lambda e, rs_c=rs_c: e.scalar_tensor_tensor(
                        ybuf[:, :], psA[:, :], tcols[:, rs_c:rs_c + 1],
                        fscr2[:, :], ALU.mult, ALU.add), inc=vc)
                else:
                    v(lambda e, rs_c=52 + i: e.tensor_scalar(
                        ybuf[:, :], psA[:, :], tcols[:, rs_c:rs_c + 1],
                        None, ALU.mult), wait=[(tc, mmN2)], inc=vc)
                bset("A", vc)
                yv = vc.n
                if dbg2_ap is not None and c == 0 and i == 0:
                    dma("gpsimd", dbg2_ap[1, :, 0:1024], ybuf[:, :],
                        wait=[(vc, yv)], inc=dDBG2)
                    if layer == 0:
                        dma("gpsimd", dbg2_ap[1, :, 1024:2048], ubc[:, :],
                            inc=dDBG2)
                        dma("gpsimd", dbg2_ap[1, :, 2048:3072], bbc[:, :],
                            inc=dDBG2)
                ln_stats(ybuf, D, 4, wait=[(vc, yv)])
                rn3 = rstd_from_var(5, 6)
                wout = [(dOUT, out_dma)] if out_dma else []
                v(lambda e: e.tensor_scalar(
                    fscr2[:, :], ybuf[:, :], tcols[:, 4:5], tcols[:, 6:7],
                    ALU.subtract, ALU.mult), wait=wout + [(sc, rn3)],
                  inc=vc)
                if layer == 0:
                    v(lambda e, g=g: e.tensor_tensor(
                        res_t[:, g * D:(g + 1) * D],
                        res_t[:, g * D:(g + 1) * D], fscr2[:, :], ALU.add),
                      inc=vc)
                else:
                    v(lambda e, g=g: e.tensor_tensor(
                        fscr2[:, :], fscr2[:, :],
                        res_t[:, g * D:(g + 1) * D], ALU.add), inc=vc)
                    oN = vc.n
                    dma("sync", out_ap[g * 128:(g + 1) * 128, :],
                        fscr2[:, :], wait=[(vc, oN)], inc=dOUT)
                    out_dma = dOUT.n
        ar.release()

    if "l0ffn" in phases:
        emit_ffn(0, early=early, region=eb_addr)

    if "l1attn" in phases:
        if pair1 and eblock_on:
            early1 = mk_pump(w_up, w_dn, "eacc1")
            early1["start"]()
        ar.mark()
        wqT = ar.alloc([128, 8 * 1024], BF16, "wqT")
        wvT = ar.alloc([128, 8 * 1024], BF16, "wvT")
        wgT = ar.alloc([128, 8 * 1024], BF16, "wgT")
        woT = ar.alloc([128, 8 * 1024], BF16, "woT")
        LT = ar.alloc([128, 16 * 128], BF16, "LT")
        pexp = ar.alloc([1, 2048], F32, "pexp")
        lnlam = ar.alloc([1, 16], F32, "lnlam")
        decs = ar.alloc([1, 16], F32, "decs")
        tri01_t = ar.alloc([128, 128], BF16, "tri01")
        gamb = ar.alloc([128, 1024], BF16, "gamb")
        betb = ar.alloc([128, 1024], BF16, "betb")
        xnT1a = ar.alloc([128, 8 * 128], BF16, "xnT1a")
        xn_st1 = ar.alloc([128, D], BF16, "xnst1")
        qst1 = ar.alloc([128, 1024], BF16, "qst1")
        vstla = ar.alloc([128, 1024], BF16, "vstla")
        gsta = ar.alloc([128, 1024], BF16, "gsta")
        yst1 = ar.alloc([128, 1024], F32, "yst1")
        ygba = ar.alloc([128, 1024], BF16, "ygba")
        ygTa = ar.alloc([128, 8 * 128], BF16, "ygTa")
        carry = ar.alloc([1, 1024], F32, "carry")
        lamrev = ar.alloc([128, 16], BF16, "lamrev")
        lam128 = ar.alloc([1, 16], F32, "lam128")
        fscr3 = ar.alloc([128, 1024], F32, "fscr3")
        ar.mark()
        prow = ar.alloc([1, 2048], BF16, "prow")
        nrow = ar.alloc([1, 2048], BF16, "nrow")
        rowf = ar.alloc([1, 1024], F32, "rowf")
        wrawb = [ar.alloc([128, 1024], BF16, f"wrawb{i}") for i in range(2)]

        dWR = [p.ctr("dWRa"), p.ctr("dWRb")]
        dDEC = p.ctr("dDEC")
        dGBr = p.ctr("dGBr")
        dTRI = p.ctr("dTRI")
        l1_guard = [(tc, tc.n), (vc, vc.n), (sc, sc.n)]
        dma("sync", tri01_t[:, :], c_tri01[:, :], wait=l1_guard, inc=dTRI)

        widx = 0
        ev_hist = []
        for wsrc, wdst in ([] if "w" in L1P_SKIP else
                           ((w_q1, wqT), (w_v1, wvT), (w_g1, wgT),
                            (w_o1, woT))):
            for r in range(8):
                wb = wrawb[widx % 2]
                wprev = [(vc, ev_hist[widx - 2])] if widx >= 2 \
                    else l1_guard
                dma("gpsimd", wb[:, :], wsrc[r * 128:(r + 1) * 128, :],
                    wait=wprev, inc=dWR[widx % 2])
                ldN = (dWR[widx % 2], dWR[widx % 2].n)
                trN = tr8(wb, wait=[ldN, (dC0, dC0.n)])
                wv4 = wdst[:, :].rearrange(
                    "p (a two c2) -> p a two c2", a=4, two=2)
                evac_pair(wv4[:, :, 0, r * 128:(r + 1) * 128],
                          wv4[:, :, 1, r * 128:(r + 1) * 128], trN)
                ev_hist.append(vc.n)
                widx += 1
        wT_vc = vc.n

        if "lam" in L1P_SKIP:
            v(lambda e: e.memset(pexp[0:1, :], 0.5), inc=vc)
            v(lambda e: e.memset(prow[0:1, :], 0.5))
            v(lambda e: e.memset(nrow[0:1, :], 0.5))
            v(lambda e: e.memset(lnlam[0:1, :], 0.5), inc=vc)
        dma("gpsimd", decs[0:1, :], dec_in.unsqueeze(0), wait=l1_guard,
            inc=dDEC)
        s(lambda e: e.activation(lnlam[0:1, :], decs[0:1, :], AF.Ln),
          wait=[(dDEC, 16)], inc=sc)
        ln_sc = sc.n
        for hf in ([] if "lam" in L1P_SKIP else range(2)):
            for hh in range(8):
                h = hf * 8 + hh
                v(lambda e, hh=hh, h=h: e.tensor_scalar(
                    rowf[0:1, hh * 128:(hh + 1) * 128], iota_t[0:1, :],
                    lnlam[0:1, h:h + 1], None, ALU.mult),
                  wait=[(sc, sc.n)])
            v(lambda e: e.drain(), inc=vc)
            a_vc = vc.n
            s(lambda e, hf=hf: e.activation(
                prow[0:1, hf * 1024:(hf + 1) * 1024], rowf[0:1, :], AF.Exp),
              wait=[(vc, a_vc)])
            s(lambda e, hf=hf: e.activation(
                nrow[0:1, hf * 1024:(hf + 1) * 1024], rowf[0:1, :], AF.Exp,
                scale=-1.0), inc=sc)
            pn_sc = sc.n
            for hh in range(8):
                h = hf * 8 + hh
                v(lambda e, hh=hh, h=h: e.tensor_scalar(
                    rowf[0:1, hh * 128:(hh + 1) * 128], iota1_t[0:1, :],
                    lnlam[0:1, h:h + 1], None, ALU.mult),
                  wait=[(sc, pn_sc)])
            v(lambda e: e.drain(), inc=vc)
            a2_vc = vc.n
            s(lambda e, hf=hf: e.activation(
                pexp[0:1, hf * 1024:(hf + 1) * 1024], rowf[0:1, :], AF.Exp),
              wait=[(vc, a2_vc)], inc=sc)
        pe_sc = sc.n
        for h in ([] if "lt" in L1P_SKIP else range(NH)):
            t_mm(psE[0:128, 0:128], nrow[0:1, h * 128:(h + 1) * 128],
                 prow[0:1, h * 128:(h + 1) * 128], start=True, stop=True,
                 wait=bwait("E") + [(sc, pe_sc), (dC0, dC0.n)], inc=tc)
            lN = tc.n
            v(lambda e, h=h: e.tensor_tensor(
                LT[:, h * 128:(h + 1) * 128], psE[0:128, 0:128],
                tri01_t[:, :], ALU.mult), wait=[(tc, lN), (dTRI, 16)],
              inc=vc)
            bset("E", vc)
        LT_vc = vc.n
        if "lt" in L1P_SKIP:
            v(lambda e: e.memset(LT[:, :], 0.25), inc=vc)
            LT_vc = vc.n
        for hf in ([] if "lr" in L1P_SKIP else range(2)):
            for hh in range(8):
                h = hf * 8 + hh
                v(lambda e, hh=hh, h=h: e.tensor_scalar(
                    rowf[0:1, hh * 128:(hh + 1) * 128], iotar_t[0:1, :],
                    lnlam[0:1, h:h + 1], None, ALU.mult),
                  wait=[(sc, sc.n)])
            v(lambda e: e.drain(), inc=vc)
            lrv = vc.n
            s(lambda e, hf=hf: e.activation(
                prow[0:1, hf * 1024:(hf + 1) * 1024], rowf[0:1, :], AF.Exp),
              wait=[(vc, lrv)], inc=sc)
            lrs = sc.n
            v(lambda e, hf=hf: e.tensor_copy(
                nrow[0:1, hf * 1024:(hf + 1) * 1024],
                prow[0:1, hf * 1024:(hf + 1) * 1024]),
              wait=[(sc, lrs)], inc=vc)
        v(lambda e: e.drain(), inc=vc)
        lrb_vc = vc.n
        for h in ([] if "lr" in L1P_SKIP else range(NH)):
            bank = ptA if h % 2 == 0 else ptEb
            p.add("tensor", lambda e, h=h, bank=bank: e.transpose(
                bank[:, 2 * (h // 2):2 * (h // 2) + 1],
                nrow[0:1, h * 128:(h + 1) * 128],
                ident_t[0:1, 0:1]),
                wait=(bwait("T") + bwait("E") + [(vc, lrb_vc)])
                if h == 0 else (),
                inc=tc if h == NH - 1 else None)
            p.add("tensor", lambda e: e.nop())
            p.add("tensor", lambda e: e.nop())
        lrt = tc.n
        if "lr" in L1P_SKIP:
            v(lambda e: e.memset(lamrev[:, :], 0.25), inc=vc)
        else:
            v(lambda e: e.tensor_copy(
                lamrev[:, 0:16].rearrange("p (a two) -> p a two", two=2)
                [:, :, 0:1],
                ptA[:, 0:16].rearrange("p (a two) -> p a two", two=2)
                [:, :, 0:1]),
              wait=[(tc, lrt)])
            v(lambda e: e.tensor_copy(
                lamrev[:, 0:16].rearrange("p (a two) -> p a two", two=2)
                [:, :, 1:2],
                ptEb[:, 0:16].rearrange("p (a two) -> p a two", two=2)
                [:, :, 0:1]))
            v(lambda e: e.drain(), inc=vc)
            bset("T", vc)
            bset("E", vc)
        v(lambda e: e.tensor_scalar(lam128[0:1, :], lnlam[0:1, :], 128.0,
                                    None, ALU.mult))
        v(lambda e: e.drain(), inc=vc)
        l128v = vc.n
        s(lambda e: e.activation(lam128[0:1, :], lam128[0:1, :], AF.Exp),
          wait=[(vc, l128v)], inc=sc)
        lam_sc = sc.n

        dma("sync", fscr3[0:1, :], gam1.unsqueeze(0), wait=l1_guard,
            inc=dGBr)
        dma("sync", carry[0:1, :], bet1.unsqueeze(0), inc=dGBr)
        gb_fin = dGBr.n
        if "gb" in L1P_SKIP:
            v(lambda e: e.memset(gamb[:, :], 1.0), inc=vc)
            v(lambda e: e.memset(betb[:, :], 0.0), inc=vc)
        for rowsrc, tgt in ([] if "gb" in L1P_SKIP else
                            ((fscr3, gamb), (carry, betb))):
            for n2 in range(2):
                t_mm(psB[:, n2 * 512:(n2 + 1) * 512], orow_f32[0:1, :],
                     rowsrc[0:1, n2 * 512:(n2 + 1) * 512],
                     start=True, stop=True,
                     wait=bwait("B") + [(dGBr, gb_fin)],
                     inc=tc if n2 == 1 else None)
            gbN = tc.n
            v(lambda e, tgt=tgt: e.tensor_copy(tgt[:, :], psB[:, :]),
              wait=[(tc, gbN)], inc=vc)
            bset("B", vc)
        gb_vc = vc.n

        for g in range(NT):
            ln_stats(res_t[:, g * D:(g + 1) * D], D, 0,
                     wait=[(dX, x_fin)] if g == 0 else ())
            rn = rstd_from_var(1, 2)
            v(lambda e, g=g: e.tensor_copy(precols[:, g * 2:g * 2 + 1],
                                           tcols[:, 0:1]), wait=[(sc, rn)])
            v(lambda e, g=g: e.tensor_copy(precols[:, g * 2 + 1:g * 2 + 2],
                                           tcols[:, 2:3]), inc=vc)
        pre1_vc = vc.n
        ar.release()

        xnT1 = [xnT1a, ar.alloc([128, 8 * 128], BF16, "xnT1b")]
        qst = [qst1, ar.alloc([128, 1024], BF16, "qstb")]
        vst_ = [vstla, ar.alloc([128, 1024], BF16, "vstb")]
        gst = [gsta, ar.alloc([128, 1024], BF16, "gstb")]
        ygb = [ygba, ar.alloc([128, 1024], BF16, "ygbb")]
        ygT = [ygTa, ar.alloc([128, 8 * 128], BF16, "ygTb")]
        yst = [yst1, ar.alloc([128, 1024], F32, "ystb")]

        pipe = (L1_LIM is None and L1_STAGE >= 99 and not L1_SKIP_SCAN
                and not L1_SKIP_CARRY and not L1_SKIP_QV
                and not L1_SKIP_TR)
        base_w = max(pre1_vc, wT_vc, LT_vc, gb_vc)

        def stage_A(g, pp):
            v(lambda e, g=g: e.tensor_scalar(
                xn_st1[:, :], res_t[:, g * D:(g + 1) * D],
                precols[:, g * 2:g * 2 + 1],
                precols[:, g * 2 + 1:g * 2 + 2],
                ALU.subtract, ALU.mult),
              wait=[(vc, base_w)], inc=vc)
            xnN = vc.n
            trN = tr8(xn_st1, wait=[(vc, xnN)])
            xt = xnT1[pp]
            xt4 = xt[:, :].rearrange("p (a two c2) -> p a two c2",
                                     a=4, two=2)
            evac_pair(xt4[:, :, 0, :], xt4[:, :, 1, :], trN)
            return vc.n

        pend_xv = [stage_A(0, 0)] if pipe else [0]
        last_sil = [0]
        _l1cnt = 0
        for b in range(B):
            for ti in range(16):
                if L1_LIM is not None and _l1cnt >= L1_LIM:
                    continue
                _l1cnt += 1
                g = b * 16 + ti
                pp = ti % 2
                xt = xnT1[pp]
                if pipe:
                    xv = pend_xv[0]
                else:
                    v(lambda e, g=g: e.tensor_scalar(
                        xn_st1[:, :], res_t[:, g * D:(g + 1) * D],
                        precols[:, g * 2:g * 2 + 1],
                        precols[:, g * 2 + 1:g * 2 + 2],
                        ALU.subtract, ALU.mult),
                      wait=[(vc, base_w)], inc=vc)
                    xnN = vc.n
                    if not L1_SKIP_TR:
                        trN = tr8(xn_st1, wait=[(vc, xnN)])
                        xt4 = xt[:, :].rearrange(
                            "p (a two c2) -> p a two c2", a=4, two=2)
                        evac_pair(xt4[:, :, 0, :], xt4[:, :, 1, :], trN)
                    else:
                        v(lambda e, xt=xt: e.memset(xt[:, :], 0.5), inc=vc)
                    xv = vc.n
                if L1_STAGE <= 1:
                    continue

                for n2 in ([] if L1_SKIP_QV else range(2)):
                    for kt in range(8):
                        t_mm(psA[:, n2 * 512:(n2 + 1) * 512],
                             xt[:, kt * 128:(kt + 1) * 128],
                             wqT[:, kt * 1024 + n2 * 512:
                                 kt * 1024 + (n2 + 1) * 512],
                             start=(kt == 0), stop=(kt == 7),
                             wait=(bwait("A") + [(vc, xv)])
                             if (kt == 0 and n2 == 0) else (),
                             inc=tc if kt == 7 else None)
                qN = tc.n
                if not L1_SKIP_QV:
                    v(lambda e, q=qst[pp]: e.tensor_copy(q[:, :], psA[:, :]),
                      wait=[(tc, qN)], inc=vc)
                    bset("A", vc)
                for n2 in ([] if L1_SKIP_QV else range(2)):
                    for kt in range(8):
                        t_mm(psB[:, n2 * 512:(n2 + 1) * 512],
                             xt[:, kt * 128:(kt + 1) * 128],
                             wvT[:, kt * 1024 + n2 * 512:
                                 kt * 1024 + (n2 + 1) * 512],
                             start=(kt == 0), stop=(kt == 7),
                             wait=bwait("B") if (kt == 0 and n2 == 0)
                             else (), inc=tc if kt == 7 else None)
                vN = tc.n
                if not L1_SKIP_QV:
                    v(lambda e, vt=vst_[pp]: e.tensor_copy(
                        vt[:, :], psB[:, :]), wait=[(tc, vN)], inc=vc)
                    bset("B", vc)
                else:
                    v(lambda e, vt=vst_[pp]: e.memset(vt[:, :], 1.0),
                      inc=vc)
                vv = vc.n
                if L1_STAGE <= 2:
                    continue
                if pipe and g < B * 16 - 1:
                    pend_xv[0] = stage_A(g + 1, (ti + 1) % 2)

                first = (ti == 0)
                for n2 in ([] if L1_SKIP_SCAN else range(2)):
                    ps, bk = (psC, "C") if n2 == 0 else (psD, "D")
                    for hh in range(8):
                        h = n2 * 8 + hh
                        t_mm(ps[:, hh * 64:(hh + 1) * 64],
                             LT[:, h * 128:(h + 1) * 128],
                             vst_[pp][:, h * 64:(h + 1) * 64],
                             start=(hh == 0),
                             stop=(first and hh == 7),
                             wait=(bwait(bk) + [(vc, vv)])
                             if hh == 0 else (),
                             inc=tc if (first and hh == 7) else None)
                    if not first:
                        for hh in range(8):
                            h = n2 * 8 + hh
                            t_mm(ps[:, hh * 64:(hh + 1) * 64],
                                 pexp[0:1, h * 128:(h + 1) * 128],
                                 carry[0:1, h * 64:(h + 1) * 64],
                                 start=False, stop=(hh == 7),
                                 inc=tc if hh == 7 else None)
                scanN = tc.n
                for n2 in ([] if L1_SKIP_CARRY else
                           ([0] if L1_CARRY_MODE == "n2_0" else range(2))):
                    for hh in range(8):
                        h = n2 * 8 + hh
                        t_mm((psE if L1_CARRY_OUT == "E" else
                              psC)[0:1, hh * 64:(hh + 1) * 64],
                             ones_bf[:, 0:1] if L1_CARRY_MODE == "lhs_ones"
                             else lamrev[:, h:h + 1],
                             vst_[pp][:, h * 64:(h + 1) * 64],
                             start=(hh == 0), stop=(first and hh == 7),
                             wait=(bwait("E") + [(sc, lam_sc),
                                                 (vc, vv)])
                             if hh == 0 else (),
                             inc=tc if (first and hh == 7) else None)
                    if not first:
                        for hh in range(8):
                            h = n2 * 8 + hh
                            t_mm(psE[0:1, hh * 64:(hh + 1) * 64],
                                 lam128[0:1, h:h + 1],
                                 carry[0:1, h * 64:(h + 1) * 64],
                                 start=False, stop=(hh == 7),
                                 inc=tc if hh == 7 else None)
                    cuN = tc.n
                    if L1_CARRY_MODE == "mm_only":
                        v(lambda e: e.nop(), wait=[(tc, cuN)], inc=vc)
                        bset("E", vc)
                        continue
                    v(lambda e, n2=n2: e.tensor_copy(
                        carry[0:1, n2 * 512:(n2 + 1) * 512],
                        psE[0:1, 0:512]), wait=[(tc, cuN)])
                    v(lambda e: e.drain(), inc=vc)
                    bset("E", vc)
                for n2 in ([] if L1_SKIP_SCAN else range(2)):
                    ps = psC if n2 == 0 else psD
                    v(lambda e, ps=ps, n2=n2, q=qst[pp], y=yst[pp]:
                      e.tensor_tensor(y[:, n2 * 512:(n2 + 1) * 512],
                                      q[:, n2 * 512:(n2 + 1) * 512],
                                      ps[:, :], ALU.mult),
                      wait=[(tc, scanN)], inc=vc)
                bset("C", vc)
                bset("D", vc)
                yv1 = vc.n
                if L1_STAGE <= 3:
                    continue

                for n2 in range(2):
                    for kt in range(8):
                        t_mm(psE[:, :], xt[:, kt * 128:(kt + 1) * 128],
                             wgT[:, kt * 1024 + n2 * 512:
                                 kt * 1024 + (n2 + 1) * 512],
                             start=(kt == 0), stop=(kt == 7),
                             wait=bwait("E") if kt == 0 else (),
                             inc=tc if kt == 7 else None)
                    gN = tc.n
                    v(lambda e, n2=n2, gs=gst[pp]: e.tensor_copy(
                        gs[:, n2 * 512:(n2 + 1) * 512], psE[:, :]),
                      wait=[(tc, gN), (sc, last_sil[0])], inc=vc)
                    bset("E", vc)
                gv = vc.n

                ln_stats(yst[pp], D, 8, wait=[(vc, yv1)])
                rn = rstd_from_var(9, 10)
                s(lambda e, gs=gst[pp]: e.activation(gs[:, :], gs[:, :],
                                                     AF.Silu),
                  wait=[(vc, gv)], inc=sc)
                sil = sc.n
                last_sil[0] = sil
                v(lambda e, y=yst[pp]: e.tensor_scalar(
                    fscr3[:, :], y[:, :], tcols[:, 8:9], tcols[:, 10:11],
                    ALU.subtract, ALU.mult), wait=[(sc, max(rn, sil))])
                v(lambda e: e.tensor_tensor(fscr3[:, :], fscr3[:, :],
                                            gamb[:, :], ALU.mult))
                v(lambda e: e.tensor_tensor(fscr3[:, :], fscr3[:, :],
                                            betb[:, :], ALU.add))
                v(lambda e, gs=gst[pp], yg=ygb[pp]: e.tensor_tensor(
                    yg[:, :], fscr3[:, :], gs[:, :], ALU.mult), inc=vc)
                ygv = vc.n
                tr2 = tr8(ygb[pp], wait=[(vc, ygv)])
                yt4 = ygT[pp][:, :].rearrange(
                    "p (a two c2) -> p a two c2", a=4, two=2)
                evac_pair(yt4[:, :, 0, :], yt4[:, :, 1, :], tr2)
                ytv = vc.n
                if L1_STAGE <= 4:
                    continue

                for n2 in range(2):
                    for kt in range(8):
                        t_mm(psA[:, n2 * 512:(n2 + 1) * 512],
                             ygT[pp][:, kt * 128:(kt + 1) * 128],
                             woT[:, kt * 1024 + n2 * 512:
                                 kt * 1024 + (n2 + 1) * 512],
                             start=(kt == 0), stop=(kt == 7),
                             wait=(bwait("A") + [(vc, ytv)])
                             if (kt == 0 and n2 == 0) else (),
                             inc=tc if kt == 7 else None)
                oN = tc.n
                ln_stats(psA, D, 12, wait=[(tc, oN)], lead=False)
                rn = rstd_from_var(13, 14)
                v(lambda e: e.tensor_scalar(
                    fscr3[:, :], psA[:, :], tcols[:, 12:13],
                    tcols[:, 14:15], ALU.subtract, ALU.mult),
                  wait=[(sc, rn)], inc=vc)
                bset("A", vc)
                v(lambda e, g=g: e.tensor_tensor(
                    res_t[:, g * D:(g + 1) * D],
                    res_t[:, g * D:(g + 1) * D], fscr3[:, :], ALU.add),
                  inc=vc)
                if early1 is not None and ti % 2 == 0:
                    early1["pump"]()
        if early1 is not None:
            early1["pump"]()
        ar.release()

    if "l1ffn" in phases:
        emit_ffn(1, early=early1, region=eb_addr)

    if resd_ap is not None:
        ar.mark()
        rd = ar.alloc([128, 1024], F32, "rdump")
        prev_rd = 0
        for g in range(NT):
            v(lambda e, g=g: e.tensor_copy(rd[:, :],
                                           res_t[:, g * D:(g + 1) * D]),
              wait=[(dDBG, prev_rd), (dX, x_fin)], inc=vc)
            rv = vc.n
            dma("sync", resd_ap[g * 128:(g + 1) * 128, :], rd[:, :],
                wait=[(vc, rv)], inc=dDBG)
            prev_rd = dDBG.n
        ar.release()

    p.add("sync", None, wait=[(dOUT, dOUT.n), (dDBG, dDBG.n),
                              (dDBG2, dDBG2.n)])
    with nc.allow_low_precision(reason="bf16 operands by design; fp32 accum"):
        p.emit()
    return nc, ctx


def shard_inputs(inputs, phases=ALL_PHASES, dump_res=False):
    consts = host_constants()
    x = np.asarray(inputs["L0_attn_x"], dtype=np.float32)
    in_maps = []
    for core in range(NCORES):
        m = dict(consts)
        m["x"] = np.ascontiguousarray(
            x[core * B:(core + 1) * B].reshape(B * T, D))
        for k, val in inputs.items():
            if k != "L0_attn_x":
                m[k] = np.ascontiguousarray(np.asarray(val, dtype=np.float32))
        in_maps.append(m)
    return in_maps


def kernel(**inputs):
    nc, ctx = build_nc()
    in_maps = shard_inputs(inputs)
    res = run_bass_kernel_spmd(nc, in_maps, core_ids=list(range(NCORES)))
    ctx.close()
    out = np.concatenate(
        [np.asarray(r["out"]).reshape(B, T, D) for r in res.results], axis=0)
    return out.astype(np.float32)


if __name__ == "__main__":
    import time
    t0 = time.time()
    nc, ctx = build_nc()
    print(f"built in {time.time()-t0:.1f}s")


# version 10 (final)
# speedup vs baseline: 1.1964x; 1.1964x over previous
import sys
sys.path.insert(0, '/opt/trn_rl_repo')
import contextlib
import numpy as np
import ml_dtypes
import concourse.bass as bass
import concourse.mybir as mybir
from concourse.bass_utils import run_bass_kernel_spmd

F32 = mybir.dt.float32
BF16 = mybir.dt.bfloat16
AF = mybir.ActivationFunctionType
ALU = mybir.AluOpType
AX = mybir.AxisListType

NCORES = 8
B = 2
T = 2048
D = 1024
NH = 16
FF = 4096
NT = (B * T) // 128
MAGIC = 1.5 * 2.0 ** 23
LN_EPS = 1e-5

PAIRMASK = []
for _i in range(0, 32, 2):
    PAIRMASK += [_i + 1, _i]

ALL_PHASES = ("l0attn", "l0ffn", "l1attn", "l1ffn")
L1_LIM = None
L1_STAGE = 99
L1_SKIP_SCAN = False
L1_SKIP_CARRY = False
L1_CARRY_MODE = "full"
L1_CARRY_OUT = "E"
L1_SKIP_QV = False
L1_SKIP_TR = False
L1P_SKIP = set()


def host_constants():
    c = {}
    c["c_ident"] = np.eye(128, dtype=ml_dtypes.bfloat16)
    inv_freq = (1.0 / 10000.0 ** (np.arange(0, 64, 2, dtype=np.float64) / 64))
    t = np.arange(T, dtype=np.float64)
    ang = t[None, :] * inv_freq[:, None]
    cos64 = np.repeat(np.cos(ang), 2, axis=0)
    sin64 = np.repeat(np.sin(ang), 2, axis=0)
    sgn = np.where(np.arange(64) % 2 == 0, -1.0, 1.0)[:, None]
    sinS = sin64 * sgn
    cos128 = np.concatenate([cos64, cos64], 0).astype(np.float32)
    sin128 = np.concatenate([sinS, sinS], 0).astype(np.float32)
    c["c_ropeqc"] = cos128
    c["c_ropeqs"] = sin128
    c["c_ropekc"] = (cos128 / 8.0).astype(np.float32)
    c["c_ropeks"] = (sin128 / 8.0).astype(np.float32)
    w = np.arange(1024)[None, :]
    p_ = np.arange(128)[:, None]
    c["c_bigm"] = (w >= p_ + 512).astype(ml_dtypes.bfloat16)
    i = np.arange(128)[None, :]
    j = np.arange(128)[:, None]
    c["c_tri01"] = (i >= j).astype(ml_dtypes.bfloat16)
    c["c_iota"] = np.arange(128, dtype=np.float32)[None, :]
    c["c_iota1"] = (np.arange(128, dtype=np.float32) + 1.0)[None, :]
    c["c_ones_bf"] = np.ones((128, 2), dtype=ml_dtypes.bfloat16)
    c["c_ones_f32"] = np.ones((128, 2), dtype=np.float32)
    c["c_onesrow_bf"] = np.ones((1, 128), dtype=ml_dtypes.bfloat16)
    c["c_onesrow_f32"] = np.ones((1, 128), dtype=np.float32)
    c["c_ident4f"] = np.eye(4, dtype=np.float32)
    c["c_magic"] = np.full((128, 1), 1.5 * 2.0 ** 23, dtype=np.float32)
    c["c_iotar"] = (127.0 - np.arange(128, dtype=np.float32))[None, :]
    return c


class Ctr:
    def __init__(self, h):
        self.h = h
        self.n = 0


class P:
    ENGINES = ("sync", "tensor", "vector", "scalar", "gpsimd")

    def __init__(self, nc, ctx):
        self.nc = nc
        self.items = {e: [] for e in self.ENGINES}
        self._waited = {}
        self._ctrs = {}
        self._ctx = ctx

    def ctr(self, name):
        if name not in self._ctrs:
            h = self._ctx.enter_context(self.nc.semaphore(name))
            self._ctrs[name] = Ctr(h)
        return self._ctrs[name]

    def add(self, eng, fn=None, wait=(), inc=None, amt=1):
        w = []
        for (c, v_) in wait:
            if v_ <= 0:
                continue
            key = (eng, id(c))
            if self._waited.get(key, -1) >= v_:
                continue
            self._waited[key] = v_
            w.append((c, v_))
        if inc is not None:
            inc.n += amt
        self.items[eng].append((fn, w, inc, amt))

    def emit(self):
        with self.nc.Block() as block:
            def mk(eng_items):
                def run(eng):
                    for fn, waits, inc, amt in eng_items:
                        for (c, v_) in waits:
                            eng.wait_ge(c.h, v_)
                        ins = fn(eng) if fn is not None else None
                        if inc is not None:
                            if ins is None:
                                ins = eng.nop()
                            ins.then_inc(inc.h, amt)
                return run
            block.sync(mk(self.items["sync"]))
            block.tensor(mk(self.items["tensor"]))
            block.vector(mk(self.items["vector"]))
            block.scalar(mk(self.items["scalar"]))
            block.gpsimd(mk(self.items["gpsimd"]))


class Arena:
    def __init__(self, nc, nbytes):
        self.nc = nc
        self.base_t = nc.alloc_sbuf_tensor("arena", [128, nbytes],
                                           mybir.dt.uint8)
        self.base_addr = nc.lookup_mloc(self.base_t).addr
        self.off = 0
        self.nbytes = nbytes
        self.stack = []
        self.count = 0
        self.peak = 0

    def alloc(self, shape, dtype, name=None):
        per_part = int(np.prod(shape[1:])) * mybir.dt.size(dtype)
        a = (self.off + 63) // 64 * 64
        assert a + per_part <= self.nbytes, \
            f"arena overflow: {a}+{per_part} > {self.nbytes} ({name})"
        self.off = a + per_part
        self.peak = max(self.peak, self.off)
        self.count += 1
        t = self.nc.alloc_sbuf_tensor_at(name or f"ar{self.count}",
                                         list(shape), dtype,
                                         offset=self.base_addr + a)
        return t

    def mark(self):
        self.stack.append(self.off)

    def release(self):
        self.off = self.stack.pop()


def build_nc(phases=ALL_PHASES, dump_res=False):
    nc = bass.Bass("TRN2", target_bir_lowering=False,
                   detect_race_conditions=False)
    ctx = contextlib.ExitStack()

    def din(name, shape, dtype=F32):
        return nc.dram_tensor(name, list(shape), dtype,
                              kind="ExternalInput").ap()

    x_in = din("x", [B * T, D])
    w_qkv = din("L0_attn_Wqkv", [D, 3 * D])
    w_proj0 = din("L0_attn_Wproj", [D, D])
    w_fc = din("L0_ffn_W_fc", [FF, D])
    ln_w = din("L0_ffn_ln_w", [FF])
    ln_b = din("L0_ffn_ln_b", [FF])
    w_pr0 = din("L0_ffn_W_proj", [D, FF])
    dec_in = din("L1_attn_decay", [NH])
    w_q1 = din("L1_attn_Wq", [D, D])
    w_v1 = din("L1_attn_Wv", [D, D])
    w_o1 = din("L1_attn_Wo", [D, D])
    w_g1 = din("L1_attn_Wg", [D, D])
    gam1 = din("L1_attn_ln_gamma", [D])
    bet1 = din("L1_attn_ln_beta", [D])
    w_up = din("L1_ffn_w_up", [FF, D])
    w_dn = din("L1_ffn_w_down", [D, FF])
    c_ident = din("c_ident", [128, 128], BF16)
    c_rqc = din("c_ropeqc", [128, T])
    c_rqs = din("c_ropeqs", [128, T])
    c_rkc = din("c_ropekc", [128, T])
    c_rks = din("c_ropeks", [128, T])
    c_bigm = din("c_bigm", [128, 1024], BF16)
    c_tri01 = din("c_tri01", [128, 128], BF16)
    c_iota = din("c_iota", [1, 128])
    c_iota1 = din("c_iota1", [1, 128])
    c_ones_bf = din("c_ones_bf", [128, 2], BF16)
    c_ones_f32 = din("c_ones_f32", [128, 2])
    c_orow_bf = din("c_onesrow_bf", [1, 128], BF16)
    c_orow_f32 = din("c_onesrow_f32", [1, 128])
    c_id4f = din("c_ident4f", [4, 4])
    c_magic = din("c_magic", [128, 1])
    c_iotar = din("c_iotar", [1, 128])

    out_ap = nc.dram_tensor("out", [B * T, D], F32, kind="ExternalOutput").ap()
    if "attdbg" in phases:
        dbg3_ap = nc.dram_tensor("dbg3", [4, 128, 4096], F32,
                                 kind="ExternalOutput").ap()
    else:
        dbg3_ap = None
    if "ffndbg" in phases:
        dbg2_ap = nc.dram_tensor("dbg2", [4, 128, 4096], F32,
                                 kind="ExternalOutput").ap()
        dbg4_ap = nc.dram_tensor("dbg4", [D, FF], BF16,
                                 kind="ExternalOutput").ap()
    else:
        dbg2_ap = None
        dbg4_ap = None
    if dump_res or "l1ffn" not in phases:
        resd_ap = nc.dram_tensor("res_dump", [B * T, D], F32,
                                 kind="ExternalOutput").ap()
    else:
        resd_ap = None

    kscr = nc.dram_tensor("kscr", [B, 8, 128, T], BF16, kind="Internal").ap()
    qscr = nc.dram_tensor("qscr", [B, 8, 128, T], BF16, kind="Internal").ap()
    vscr = nc.dram_tensor("vscr", [B, 16, 128, 1040], BF16,
                          kind="Internal").ap()
    w1t_scr = [nc.dram_tensor(f"w1t{i}", [D, FF], BF16, kind="Internal").ap()
               for i in range(2)]

    res_t = nc.alloc_sbuf_tensor("res", [128, NT * D], BF16)
    ident_t = nc.alloc_sbuf_tensor("ident", [128, 128], BF16)
    ones_bf = nc.alloc_sbuf_tensor("onesbf", [128, 2], BF16)
    ones_f32 = nc.alloc_sbuf_tensor("onesf32", [128, 2], F32)
    orow_bf = nc.alloc_sbuf_tensor("orowbf", [1, 128], BF16)
    orow_f32 = nc.alloc_sbuf_tensor("orowf32", [1, 128], F32)
    iota_t = nc.alloc_sbuf_tensor("iota", [1, 128], F32)
    iota1_t = nc.alloc_sbuf_tensor("iota1", [1, 128], F32)
    precols = nc.alloc_sbuf_tensor("precols", [128, NT * 2], F32)
    tcols = nc.alloc_sbuf_tensor("tcols", [128, 64], F32)
    bnsc = nc.alloc_sbuf_tensor("bnsc", [128, 12], F32)
    id4f_t = nc.alloc_sbuf_tensor("id4f", [4, 4], F32)
    magic_t = nc.alloc_sbuf_tensor("magic", [128, 1], F32)
    iotar_t = nc.alloc_sbuf_tensor("iotar", [1, 128], F32)

    psA = nc.alloc_psum_tensor("psA", [128, 1024], F32)
    psB = nc.alloc_psum_tensor("psB", [128, 1024], F32)
    psC = nc.alloc_psum_tensor("psC", [128, 512], F32)
    psD = nc.alloc_psum_tensor("psD", [128, 512], F32)
    ptA = nc.alloc_psum_tensor("ptA", [128, 1024], BF16)
    psE = nc.alloc_psum_tensor("psE", [128, 512], F32)

    avail = nc.sbuf_bytes_remaining - 128
    ar = Arena(nc, avail)

    p = P(nc, ctx)
    tc = p.ctr("tc")
    vc = p.ctr("vc")
    sc = p.ctr("sc")
    dC0 = p.ctr("dC0")
    dX = p.ctr("dX")
    dOUT = p.ctr("dOUT")
    dDBG = p.ctr("dDBG")
    dDBG2 = p.ctr("dDBG2")

    def t_mm(out, lhsT, rhs, start, stop, wait=(), inc=None):
        p.add("tensor", lambda e, o=out, l=lhsT, r=rhs, s_=start, q=stop:
              e.matmul(o, l, r, start=s_, stop=q), wait=wait, inc=inc)

    ptEb = psE[:, :].bitcast(BF16)

    def t_tr(out, in_, wait=(), inc=None):
        p.add("tensor", lambda e, o=out, i=in_:
              e.transpose(o, i, ident_t[:, :]), wait=wait, inc=inc)

    def tr8(src, wait=()):
        w0 = list(wait) + bwait("T") + bwait("E")
        for k in range(8):
            tgt = ptA if k % 2 == 0 else ptEb
            p.add("tensor",
                  lambda e, k=k, tgt=tgt, src=src: e.transpose(
                      tgt[:, k * 128:(k + 1) * 128],
                      src[:, k * 128:(k + 1) * 128], ident_t[:, :]),
                  wait=w0 if k == 0 else (),
                  inc=tc if k == 7 else None)
        return tc.n

    def evac_pair(dst_even, dst_odd, trN, inc=None):
        pa = ptA[:, :].rearrange("p (a two c2) -> p a two c2", a=4, two=2)
        pe = ptEb[:, :].rearrange("p (a two c2) -> p a two c2", a=4, two=2)
        v(lambda e: e.tensor_copy(dst_even, pa[:, :, 0, :]),
          wait=[(tc, trN)])
        v(lambda e: e.tensor_copy(dst_odd, pe[:, :, 1, :]), inc=inc or vc)
        bset("T", inc or vc)
        bset("E", inc or vc)

    def v(fn, wait=(), inc=None):
        p.add("vector", fn, wait=wait, inc=inc)

    def s(fn, wait=(), inc=None):
        p.add("scalar", fn, wait=wait, inc=inc)

    def dma(eng, out, in_, wait=(), inc=None, slow=False):
        if slow:
            p.add(eng, lambda e, o=out, i=in_: e.dma_start(
                o, i, allow_slow_non_contiguous=True), wait=wait,
                inc=inc, amt=16)
        else:
            p.add(eng, lambda e, o=out, i=in_: e.dma_start(o, i), wait=wait,
                  inc=inc, amt=16)

    bankg = {k: (None, 0) for k in "ABCDET"}

    def bwait(bk):
        c, n0 = bankg[bk]
        return [(c, n0)] if c is not None else []

    def bset(bk, c):
        bankg[bk] = (c, c.n)

    def ln_stats(src, width, c0, wait=(), lead=True):
        nch = (width + 511) // 512
        if lead:
            v(lambda e: e.drain())
        for k in range(nch):
            v(lambda e, sa=src, a=k * 512, b_=min(width, k * 512 + 512), kk=k:
              e.bn_stats(bnsc[:, kk * 6:kk * 6 + 6], sa[:, a:b_]),
              wait=wait if k == 0 else ())
        v(lambda e: e.drain())
        v(lambda e, n=nch, c=c0:
          e.bn_aggr(tcols[:, c:c + 2], bnsc[:, 0:n * 6]))
        v(lambda e: e.drain())

    _rstd_ln_sc = [0]

    def rstd_from_var(c_var, c_out, eps=LN_EPS, lead=False):
        if lead:
            v(lambda e: e.drain())
        v(lambda e: e.tensor_scalar(tcols[:, 62:63],
                                    tcols[:, c_var:c_var + 1], float(eps),
                                    None, ALU.add),
          wait=[(sc, _rstd_ln_sc[0])], inc=vc)
        v(lambda e: e.drain())
        v(lambda e: e.reciprocal(tcols[:, 63:64], tcols[:, 62:63]))
        v(lambda e: e.drain(), inc=vc)
        n = vc.n
        s(lambda e: e.activation(tcols[:, c_out:c_out + 1],
                                 tcols[:, 63:64], AF.Sqrt), wait=[(vc, n)],
          inc=sc)
        _rstd_ln_sc[0] = sc.n
        m = sc.n
        return m

    pair0 = ("l0attn" in phases) and ("l0ffn" in phases) and \
        ("xnonly" not in phases) and ("l0proj" not in phases)
    pair1 = ("l1attn" in phases) and ("l1ffn" in phases)
    eblock_on = pair0 or pair1
    early = None
    early1 = None
    eb_addr = None
    if eblock_on:
        ebufs = [ar.alloc([128, 1024], F32, f"ebuf{j}") for j in range(4)]
        eb_addr = [nc.lookup_mloc(t).addr for t in ebufs]
        assert eb_addr[1] == eb_addr[0] + 4096 and \
            eb_addr[3] == eb_addr[2] + 4096
        dEB = [p.ctr(f"dEB{j}") for j in range(4)]
        eaccs = ar.alloc([128, 4], F32, "eaccs")
        _eacc_off = [0]

        def mk_pump(w1_ap, w2_ap, acc_name):
            e_acc = eaccs[:, _eacc_off[0]:_eacc_off[0] + 2]
            _eacc_off[0] += 2
            tiles = [(w1_ap, r * 128, 0, 0) for r in range(32)] + \
                [(w2_ap, dt * 128, q * 1024, 1)
                 for dt in range(8) for q in range(4)]
            st = {"grp": 0, "cons_vc": 0, "pending": None,
                  "acc": e_acc, "addr": eb_addr}

            def start():
                v(lambda e: e.memset(e_acc[:, :], 0.0), inc=vc)
                st["cons_vc"] = vc.n

            def pump():
                if st["pending"] is not None:
                    for j, (nn, col) in enumerate(st["pending"]):
                        for k in range(2):
                            v(lambda e, j=j, k=k: e.tensor_reduce(
                                tcols[:, 8 + k:9 + k],
                                ebufs[j][:, k * 512:(k + 1) * 512],
                                AX.X, ALU.add, apply_absolute_value=True),
                              wait=[(dEB[j], nn)] if k == 0 else ())
                        v(lambda e: e.drain())
                        v(lambda e: e.tensor_tensor(
                            tcols[:, 8:9], tcols[:, 8:9], tcols[:, 9:10],
                            ALU.add))
                        v(lambda e: e.drain())
                        v(lambda e, col=col, e_acc=e_acc: e.tensor_tensor(
                            e_acc[:, col:col + 1], e_acc[:, col:col + 1],
                            tcols[:, 8:9], ALU.add), inc=vc)
                    st["cons_vc"] = vc.n
                    st["pending"] = None
                g = st["grp"]
                if g < 16:
                    pend = []
                    for j in range(4):
                        wsrc, r0, c0, col = tiles[g * 4 + j]
                        dma("gpsimd", ebufs[j][:, :],
                            wsrc[r0:r0 + 128, c0:c0 + 1024],
                            wait=[(vc, st["cons_vc"])] if j == 0 else (),
                            inc=dEB[j])
                        pend.append((dEB[j].n, col))
                    st["pending"] = pend
                    st["grp"] = g + 1
            st["pump"] = pump
            st["start"] = start
            return st
        if pair0:
            early = mk_pump(w_fc, w_pr0, "eacc0")
            early["start"]()

    for src, tgt in ((c_ident, ident_t), (c_ones_bf, ones_bf),
                     (c_ones_f32, ones_f32), (c_orow_bf, orow_bf),
                     (c_orow_f32, orow_f32), (c_iota, iota_t),
                     (c_iota1, iota1_t), (c_id4f, id4f_t),
                     (c_magic, magic_t), (c_iotar, iotar_t)):
        dma("sync", tgt[:, :], src[:, :], inc=dC0)
    for g in range(NT):
        dma("gpsimd", res_t[:, g * D:(g + 1) * D],
            x_in[g * 128:(g + 1) * 128, :], inc=dX)
    x_fin = dX.n

    l0attn_on = ("l0attn" in phases) or ("l0proj" in phases) or \
        ("xnonly" in phases)
    attn_sub = "l0attn" in phases
    if l0attn_on:
        ar.mark()
        wproj_b = ar.alloc([128, 8 * D], BF16, "wprojb")
        mbias = ar.alloc([128, 32], F32, "mbias")
        m2q = ar.alloc([1, 16], F32, "m2q")
        m2k = ar.alloc([1, 16], F32, "m2k")
        fscr = ar.alloc([128, 1024], F32, "fscr")
        bigm_t = ar.alloc([128, 1024], BF16, "bigm")
        dWP = p.ctr("dWP")
        dma("sync", bigm_t[:, :], c_bigm[:, :], inc=dC0)
        for kt in range(8):
            dma("gpsimd", wproj_b[:, kt * D:(kt + 1) * D],
                w_proj0[kt * 128:(kt + 1) * 128, :], inc=dWP)
        wp_fin = dWP.n

        v(lambda e: e.memset(m2q[:, :], 0.0))
        v(lambda e: e.memset(m2k[:, :], 0.0), inc=vc)

        ar.mark()
        ropeqc = ar.alloc([128, T], F32, "ropeqc")
        ropeqs = ar.alloc([128, T], F32, "ropeqs")
        ropekc = ar.alloc([128, T], F32, "ropekc")
        ropeks = ar.alloc([128, T], F32, "ropeks")
        wqkv_b = ar.alloc([128, 8 * 3 * D], BF16, "wqkvb")
        xnT = ar.alloc([128, 8 * 512], BF16, "xnT")
        xn_stage = ar.alloc([128, D], BF16, "xnstage")
        raw_s = ar.alloc([128, 512], BF16, "raws")
        sh_s = ar.alloc([128, 512], BF16, "shs")
        t1_s = ar.alloc([128, 512], BF16, "t1s")
        t2_s = ar.alloc([128, 512], BF16, "t2s")
        rope_o = [ar.alloc([128, 512], BF16, f"ropeo{i}") for i in range(2)]
        sq_s = ar.alloc([128, 512], BF16, "sqs")
        vstage = [ar.alloc([128, 1040], BF16, f"vstg{i}") for i in range(2)]

        dWQ = p.ctr("dWQ")
        dSpro = [p.ctr("dSpro0"), p.ctr("dSpro1")]
        dSpv = [p.ctr("dSpv0"), p.ctr("dSpv1")]
        for src, tgt in ((c_rqc, ropeqc), (c_rqs, ropeqs), (c_rkc, ropekc),
                         (c_rks, ropeks)):
            dma("sync", tgt[:, :], src[:, :], inc=dC0)
        for kt in range(8):
            dma("gpsimd", wqkv_b[:, kt * 3 * D:(kt + 1) * 3 * D],
                w_qkv[kt * 128:(kt + 1) * 128, :], inc=dWQ)
        wq_fin = dWQ.n

        ro_spill = {}
        vst_spill = {}
        mbias_vc = {}

        def proj_stats(b, c):
            for i in range(4):
                g = b * 16 + c * 4 + i
                ln_stats(res_t[:, g * D:(g + 1) * D], D, 0,
                         wait=[(dX, x_fin)])
                rn = rstd_from_var(1, 2)
                v(lambda e, g=g: e.tensor_copy(
                    precols[:, g * 2:g * 2 + 1], tcols[:, 0:1]),
                  wait=[(sc, rn)])
                v(lambda e, g=g: e.tensor_copy(
                    precols[:, g * 2 + 1:g * 2 + 2], tcols[:, 2:3]),
                  inc=vc)

        pseq = [(b, c) for b in ([0] if "xnonly" in phases else range(B))
                for c in ([0] if "xnonly" in phases else range(4))]
        proj_stats(*pseq[0])
        for pidx, (b, c) in enumerate(pseq):
            if True:
                for i in range(4):
                    g = b * 16 + c * 4 + i
                    v(lambda e, g=g: e.tensor_scalar(
                        xn_stage[:, :], res_t[:, g * D:(g + 1) * D],
                        precols[:, g * 2:g * 2 + 1],
                        precols[:, g * 2 + 1:g * 2 + 2],
                        ALU.subtract, ALU.mult), inc=vc)
                    xn_vc = vc.n
                    trN = tr8(xn_stage,
                              wait=[(vc, xn_vc), (dC0, dC0.n)])
                    xv4 = xnT[:, :].rearrange(
                        "p (a two c2) -> p a two c2", a=4, two=2)
                    evac_pair(xv4[:, :, 0, i * 128:(i + 1) * 128],
                              xv4[:, :, 1, i * 128:(i + 1) * 128], trN)
                xnT_vc = vc.n

                if dbg3_ap is not None and b == 0 and c == 0:
                    dma("gpsimd", dbg3_ap[0, :, 0:4096], xnT[:, :],
                        wait=[(vc, xnT_vc)], inc=dDBG2)
                if "xnonly" in phases:
                    continue
                    for w_ in range(3):
                        dma("gpsimd", dbg3_ap[1 + w_, :, 0:4096],
                            wqkv_b[:, w_ * 4096:(w_ + 1) * 4096],
                            wait=[(dWQ, wq_fin)], inc=dDBG2)
                for kind, colbase, tabc, tabs, scrap, m2 in (
                        ("q", 0, ropeqc, ropeqs, qscr, m2q),
                        ("k", D, ropekc, ropeks, kscr, m2k)):
                    for et in range(8):
                        ps, bk = (psC, "C") if et % 2 == 0 else (psD, "D")
                        for dt in range(8):
                            t_mm(ps[:, :],
                                 wqkv_b[:, dt * 3 * D + colbase + et * 128:
                                        dt * 3 * D + colbase + (et + 1) * 128],
                                 xnT[:, dt * 512:(dt + 1) * 512],
                                 start=(dt == 0), stop=(dt == 7),
                                 wait=(bwait(bk) +
                                       [(vc, xnT_vc), (dWQ, wq_fin),
                                        (dC0, dC0.n)])
                                 if dt == 0 else (),
                                 inc=tc if dt == 7 else None)
                        mmN = tc.n
                        wrs = ([(vc, ro_spill.get("rope_vc", 0))]
                               if ro_spill.get("rope_vc") else [])
                        s(lambda e, ps=ps: e.copy(raw_s[:, :], ps[:, :]),
                          wait=[(tc, mmN)] + wrs, inc=sc)
                        s(lambda e, ps=ps: e.square(sq_s[:, :], ps[:, :]),
                          inc=sc)
                        sqN = sc.n
                        v(lambda e: e.stream_shuffle(sh_s[:, :], raw_s[:, :],
                                                     PAIRMASK),
                          wait=[(sc, sqN), (dC0, dC0.n)])
                        v(lambda e: e.drain())
                        v(lambda e, ps=ps, tb=tabc, c=c: e.tensor_tensor(
                            t1_s[:, :], ps[:, :],
                            tb[:, c * 512:(c + 1) * 512], ALU.mult), inc=vc)
                        bset(bk, vc)
                        v(lambda e, tb=tabs, c=c: e.tensor_tensor(
                            t2_s[:, :], sh_s[:, :],
                            tb[:, c * 512:(c + 1) * 512], ALU.mult))
                        ro = rope_o[et % 2]
                        key = et % 2
                        wsp = ([ro_spill[key]] if key in ro_spill else [])
                        v(lambda e, ro=ro: e.tensor_tensor(
                            ro[:, :], t1_s[:, :], t2_s[:, :], ALU.add),
                          wait=wsp, inc=vc)
                        ropeN = vc.n
                        ro_spill["rope_vc"] = ropeN
                        for half in range(2):
                            h = 2 * et + half
                            t_mm(psE[0:1, 0:512],
                                 ones_bf[half * 64:(half + 1) * 64, 0:1],
                                 sq_s[half * 64:(half + 1) * 64, :],
                                 start=True, stop=True,
                                 wait=bwait("E") + [(sc, sqN)], inc=tc)
                            nmN = tc.n
                            v(lambda e: e.tensor_reduce(
                                tcols[0:1, 4:5], psE[0:1, 0:512], AX.X,
                                ALU.max), wait=[(tc, nmN)], inc=vc)
                            bset("E", vc)
                            v(lambda e: e.drain())
                            v(lambda e, m2=m2, h=h: e.tensor_tensor(
                                m2[0:1, h:h + 1], m2[0:1, h:h + 1],
                                tcols[0:1, 4:5], ALU.max), inc=vc)
                        dma("sync", scrap[b, et, :, c * 512:(c + 1) * 512],
                            ro[:, :], wait=[(vc, ropeN)], inc=dSpro[key])
                        ro_spill[key] = (dSpro[key], dSpro[key].n)
                for i in range(4):
                    kt = c * 4 + i
                    vst = vstage[i % 2]
                    for n2 in range(2):
                        ps, bk = (psC, "C") if n2 == 0 else (psD, "D")
                        for dt in range(8):
                            t_mm(ps[:, :],
                                 xnT[:, dt * 512 + i * 128:
                                     dt * 512 + (i + 1) * 128],
                                 wqkv_b[:, dt * 3 * D + 2 * D + n2 * 512:
                                        dt * 3 * D + 2 * D + (n2 + 1) * 512],
                                 start=(dt == 0), stop=(dt == 7),
                                 wait=bwait(bk) if dt == 0 else (),
                                 inc=tc if dt == 7 else None)
                        mmN = tc.n
                        wsp = ([vst_spill[i % 2]]
                               if (i % 2) in vst_spill and n2 == 0 else [])
                        v(lambda e, ps=ps, vst=vst, n2=n2: e.tensor_copy(
                            vst[:, :].rearrange("p (h q) -> p h q", h=16)
                            [:, n2 * 8:(n2 + 1) * 8, 0:64],
                            ps[:, :].rearrange("p (h q) -> p h q", h=8)),
                          wait=wsp + [(tc, mmN)], inc=vc)
                        bset(bk, vc)
                    v(lambda e, vst=vst: e.memset(
                        vst[:, :].rearrange("p (h q) -> p h q", h=16)
                        [:, :, 64:65], 1.0), inc=vc)
                    vN = vc.n
                    dma("sync", vscr[b, kt, :, :], vst[:, :],
                        wait=[(vc, vN)], inc=dSpv[i % 2])
                    vst_spill[i % 2] = (dSpv[i % 2], dSpv[i % 2].n)
                if early is not None:
                    early["pump"]()
                if pidx + 1 < len(pseq):
                    proj_stats(*pseq[pidx + 1])

            if c != 3 and "xnonly" not in phases:
                continue
            v(lambda e: e.tensor_tensor(tcols[0:1, 16:32], m2q[0:1, :],
                                        m2k[0:1, :], ALU.mult))
            v(lambda e: e.drain(), inc=vc)
            mvN = vc.n
            s(lambda e: e.activation(tcols[0:1, 16:32], tcols[0:1, 16:32],
                                     AF.Sqrt), wait=[(vc, mvN)], inc=sc)
            msN = sc.n
            v(lambda e: e.tensor_scalar(tcols[0:1, 32:48], tcols[0:1, 16:32],
                                        -0.125, None, ALU.mult),
              wait=[(sc, msN)])
            v(lambda e: e.drain())
            v(lambda e: e.tensor_copy(raw_s[0:1, 0:16], tcols[0:1, 32:48]))
            v(lambda e: e.drain(), inc=vc)
            mrN = vc.n
            t_mm(psE[0:128, 0:16], orow_bf[0:1, :], raw_s[0:1, 0:16],
                 start=True, stop=True, wait=bwait("E") + [(vc, mrN)],
                 inc=tc)
            mtN = tc.n
            v(lambda e, b=b: e.tensor_copy(mbias[:, b * 16:(b + 1) * 16],
                                           psE[0:128, 0:16]),
              wait=[(tc, mtN)], inc=vc)
            bset("E", vc)
            mbias_vc[b] = vc.n
            if b == 0:
                v(lambda e: e.memset(m2q[:, :], 0.0))
                v(lambda e: e.memset(m2k[:, :], 0.0), inc=vc)

        proj_tc, proj_vc, proj_sc = tc.n, vc.n, sc.n
        ar.release()

        if "l0proj" in phases:
            dbg = nc.dram_tensor("dbg", [2, B, 8, 128, T], BF16,
                                 kind="ExternalOutput").ap()
            dbgv = nc.dram_tensor("dbgv", [B, 16, 128, 1040], BF16,
                                  kind="ExternalOutput").ap()
            spw = [(dSpro[0], dSpro[0].n), (dSpro[1], dSpro[1].n),
                   (dSpv[0], dSpv[0].n), (dSpv[1], dSpv[1].n)]
            for b in range(B):
                for et in range(8):
                    dma("sync", dbg[0, b, et, :, :], qscr[b, et, :, :],
                        wait=spw, inc=dDBG)
                    dma("sync", dbg[1, b, et, :, :], kscr[b, et, :, :],
                        inc=dDBG)
                for kt in range(16):
                    dma("sync", dbgv[b, kt, :, :], vscr[b, kt, :, :],
                        inc=dDBG)

        kwin = [ar.alloc([128, T], BF16, f"kwin{i}") for i in range(2)]
        qwin = [ar.alloc([128, 512], BF16, f"qwin{i}") for i in range(2)]
        vwin = [ar.alloc([128, 1040], BF16, f"vwin{i}") for i in range(16)]
        ptile = [ar.alloc([128, 512], BF16, f"ptile{i}") for i in range(2)]
        rrow = ar.alloc([1, 512], BF16, "rrow")
        bcb = ar.alloc([64, 512], BF16, "bcb")
        attnT = ar.alloc([128, 8 * 512], BF16, "attnT")
        alias_w = [(tc, proj_tc), (vc, proj_vc), (sc, proj_sc)]

        dKW = [p.ctr("dKW0"), p.ctr("dKW1")]
        dQW = [p.ctr("dQW0"), p.ctr("dQW1")]
        dVW = p.ctr("dVW")
        spfin = [(dSpro[0], dSpro[0].n), (dSpro[1], dSpro[1].n),
                 (dSpv[0], dSpv[0].n), (dSpv[1], dSpv[1].n)]
        prev_pv_tc = 0
        for b in range(B) if attn_sub else []:
            kq_used = {}
            prev_y_tc = 0
            for kt in range(16):
                dma("sync", vwin[kt][:, :], vscr[b, kt, :, :],
                    wait=alias_w + spfin + [(tc, prev_pv_tc)], inc=dVW)
            vw_fin = dVW.n
            for qc in range(4):
                ext = (qc + 1) * 512
                for et in range(8):
                    kbuf = kwin[et % 2]
                    qbuf = qwin[et % 2]
                    wprev = ([(tc, kq_used[et % 2])]
                             if (et % 2) in kq_used else [])
                    dma("sync", kbuf[:, 0:ext], kscr[b, et, :, 0:ext],
                        wait=alias_w + wprev + spfin, inc=dKW[et % 2])
                    dma("sync", qbuf[:, :],
                        qscr[b, et, :, qc * 512:(qc + 1) * 512],
                        wait=wprev, inc=dQW[et % 2])
                    kq_ready = [(dKW[et % 2], dKW[et % 2].n),
                                (dQW[et % 2], dQW[et % 2].n)]
                    if dbg3_ap is not None and b == 0 and qc == 0 \
                            and et == 0:
                        dma("gpsimd", dbg3_ap[0, :, 0:32], mbias[:, :],
                            wait=[(vc, mbias_vc[B - 1])], inc=dDBG2)
                        dma("gpsimd", dbg3_ap[2, :, 0:2048], kbuf[:, :],
                            wait=kq_ready, inc=dDBG2)
                        dma("gpsimd", dbg3_ap[3, :, 0:512], qbuf[:, :],
                            wait=kq_ready, inc=dDBG2)
                    for half in range(2):
                        h = 2 * et + half
                        pv = psA[:, half * 512:(half + 1) * 512]
                        nkt = qc * 4 + 4
                        for kt in range(nkt):
                            sps, sbk = ((psC, "C") if kt % 2 == 0
                                        else (psD, "D"))
                            t_mm(sps[:, :],
                                 kbuf[half * 64:(half + 1) * 64,
                                      kt * 128:(kt + 1) * 128],
                                 qbuf[half * 64:(half + 1) * 64, :],
                                 start=True, stop=True,
                                 wait=bwait(sbk) + kq_ready, inc=tc)
                            scN = tc.n
                            pt = ptile[kt % 2]
                            s(lambda e, sps=sps, pt=pt, b=b, h=h:
                              e.activation(pt[:, :], sps[:, :], AF.Exp,
                                           bias=mbias[:,
                                                      b * 16 + h:
                                                      b * 16 + h + 1]),
                              wait=[(tc, scN), (vc, mbias_vc[b])], inc=sc)
                            bset(sbk, sc)
                            pctr, pn = sc, sc.n
                            if kt >= qc * 4:
                                off = (kt - qc * 4) * 128
                                st0 = 512 - off
                                v(lambda e, pt=pt, st0=st0: e.tensor_tensor(
                                    pt[:, :], pt[:, :],
                                    bigm_t[:, st0:st0 + 512], ALU.mult),
                                  wait=[(sc, pn)], inc=vc)
                                pctr, pn = vc, vc.n
                            t_mm(pv[0:65, :],
                                 vwin[kt][:, h * 65:(h + 1) * 65], pt[:, :],
                                 start=(kt == 0), stop=(kt == nkt - 1),
                                 wait=(bwait("A") if kt == 0 else []) +
                                 [(pctr, pn), (dVW, vw_fin)],
                                 inc=tc if kt == nkt - 1 else None)
                        pvN = tc.n
                        v(lambda e, pv=pv: e.reciprocal(rrow[0:1, :],
                                                        pv[64:65, :]),
                          wait=[(tc, pvN)])
                        v(lambda e: e.drain(), inc=vc)
                        rN = vc.n
                        t_mm(psB[0:64, half * 512:(half + 1) * 512],
                             orow_bf[0:1, 0:64], rrow[0:1, :],
                             start=True, stop=True,
                             wait=bwait("B") + [(vc, rN)], inc=tc)
                        bcN = tc.n
                        s(lambda e, half=half: e.copy(
                            bcb[:, :],
                            psB[0:64, half * 512:(half + 1) * 512]),
                          wait=[(tc, bcN)], inc=sc)
                        bset("B", sc)
                        bN = sc.n
                        v(lambda e, pv=pv, et=et, half=half:
                          e.tensor_tensor(
                              attnT[half * 64:(half + 1) * 64,
                                    et * 512:(et + 1) * 512],
                              pv[0:64, :], bcb[:, :], ALU.mult),
                          wait=[(sc, bN), (tc, prev_y_tc)], inc=vc)
                        bset("A", vc)
                    kq_used[et % 2] = tc.n
                attn_vc = vc.n
                if dbg3_ap is not None and b == 0 and qc == 0:
                    dma("gpsimd", dbg3_ap[1, :, 0:4096], attnT[:, :],
                        wait=[(vc, attn_vc)], inc=dDBG2)
                for i in range(4):
                    g = b * 16 + qc * 4 + i
                    for n2 in range(2):
                        for kt in range(8):
                            t_mm(psB[:, n2 * 512:(n2 + 1) * 512],
                                 attnT[:, kt * 512 + i * 128:
                                       kt * 512 + (i + 1) * 128],
                                 wproj_b[:, kt * D + n2 * 512:
                                         kt * D + (n2 + 1) * 512],
                                 start=(kt == 0), stop=(kt == 7),
                                 wait=(bwait("B") +
                                       [(vc, attn_vc), (dWP, wp_fin)])
                                 if kt == 0 else (),
                                 inc=tc if kt == 7 else None)
                    yN = tc.n
                    ln_stats(psB, D, 4, wait=[(tc, yN)], lead=False)
                    rn = rstd_from_var(5, 6)
                    v(lambda e: e.tensor_scalar(
                        fscr[:, :], psB[:, :], tcols[:, 4:5], tcols[:, 6:7],
                        ALU.subtract, ALU.mult), wait=[(sc, rn)], inc=vc)
                    bset("B", vc)
                    v(lambda e, g=g: e.tensor_tensor(
                        res_t[:, g * D:(g + 1) * D],
                        res_t[:, g * D:(g + 1) * D], fscr[:, :], ALU.add),
                      inc=vc)
                prev_y_tc = tc.n
                if early is not None:
                    early["pump"]()
            prev_pv_tc = tc.n
        if early is not None:
            early["pump"]()
        ar.release()

    def emit_ffn(layer, early=None, region=None, pre_vc_done=None):
        if early is not None:
            region = early["addr"]
        ar.mark()
        w2res = ar.alloc([128, 32 * 1024], BF16, "w2res")
        lnwc = ar.alloc([128, 32], F32, "lnwc")
        lnbc_ = ar.alloc([128, 32], F32, "lnbcf")
        lnbcb = ar.alloc([128, 32], BF16, "lnbcb")
        rgb = ar.alloc([128, 2], F32, "rgb")
        acc_c = ar.alloc([128, 2], F32, "accc")
        trowA = ar.alloc([1, 512], F32, "trowA")
        trowB = ar.alloc([1, 512], F32, "trowB")
        trow_b = ar.alloc([1, 2048], BF16, "trowb")

        dBR = [p.ctr(f"dBR{layer}{j}") for j in range(4)]
        dB2 = [p.ctr(f"dB2{layer}a"), p.ctr(f"dB2{layer}b")]
        dW1S = p.ctr(f"dW1S{layer}")
        dW1 = [p.ctr(f"dW1{layer}a"), p.ctr(f"dW1{layer}b")]
        dLNW = p.ctr("dLNW") if layer == 0 else None
        dLNB = p.ctr("dLNB") if layer == 0 else None
        w1_in = w_fc if layer == 0 else w_up
        w2_in = w_pr0 if layer == 0 else w_dn
        w1scr = w1t_scr[layer]
        eps_g = 1e-6 if layer == 0 else 0.0

        ar.mark()
        if region is not None:
            braw = [nc.alloc_sbuf_tensor_at(f"brawV{layer}{i}",
                                            [128, 1024], F32,
                                            offset=region[i])
                    for i in range(4)]
            braw2 = [nc.alloc_sbuf_tensor_at(f"braw2V{layer}{i}",
                                             [128, 2048], F32,
                                             offset=region[2 * i])
                     for i in range(2)]
        else:
            braw = [ar.alloc([128, 1024], F32, f"braw{layer}{i}")
                    for i in range(2)]
            braw2 = [ar.alloc([128, 2048], F32, f"braw2{layer}{i}")
                     for i in range(2)]
        if early is not None:
            acc_c = early["acc"]
        tful = ar.alloc([128, 2048], F32, "tful")
        qful = ar.alloc([128, 2048], BF16, "qful")

        phase_guard = [(tc, tc.n), (vc, vc.n), (sc, sc.n)]
        if early is None:
            v(lambda e: e.memset(acc_c[:, :], 0.0), inc=vc)
            for r in range(32):
                br = braw[r % 2]
                wprev = [(vc, vc.n)] if r >= 2 else phase_guard
                dma("gpsimd", br[:, :], w1_in[r * 128:(r + 1) * 128, :],
                    wait=wprev, inc=dBR[r % 2])
                ldN = (dBR[r % 2], dBR[r % 2].n)
                for k in range(2):
                    v(lambda e, br=br, k=k: e.tensor_reduce(
                        tcols[:, 8 + k:9 + k], br[:, k * 512:(k + 1) * 512],
                        AX.X, ALU.add, apply_absolute_value=True),
                      wait=[ldN] if k == 0 else ())
                v(lambda e: e.drain())
                v(lambda e: e.tensor_tensor(tcols[:, 8:9], tcols[:, 8:9],
                                            tcols[:, 9:10], ALU.add))
                v(lambda e: e.drain())
                v(lambda e: e.tensor_tensor(acc_c[:, 0:1], acc_c[:, 0:1],
                                            tcols[:, 8:9], ALU.add), inc=vc)
            for r in range(16):
                br2 = braw2[r % 2]
                wprev = [(vc, vc.n)] if r >= 2 else \
                    (phase_guard + [(vc, vc.n)])
                dma("gpsimd", br2[:, :],
                    w2_in[(r // 2) * 128:(r // 2 + 1) * 128,
                          (r % 2) * 2048:(r % 2 + 1) * 2048], wait=wprev,
                    inc=dB2[r % 2])
                ldN = (dB2[r % 2], dB2[r % 2].n)
                for k in range(4):
                    v(lambda e, br2=br2, k=k: e.tensor_reduce(
                        tcols[:, 8 + (k % 2):9 + (k % 2)],
                        br2[:, k * 512:(k + 1) * 512],
                        AX.X, ALU.add, apply_absolute_value=True),
                      wait=[ldN] if k == 0 else ())
                    v(lambda e: e.drain())
                    v(lambda e, k=k: e.tensor_tensor(
                        acc_c[:, 1:2], acc_c[:, 1:2],
                        tcols[:, 8 + (k % 2):9 + (k % 2)], ALU.add),
                      inc=vc if k == 3 else None)
            g_vc = vc.n
        else:
            g_vc = early["cons_vc"]
        for wi in range(2):
            t_mm(psE[0:1, 0:1], ones_f32[:, 0:1], acc_c[:, wi:wi + 1],
                 start=True, stop=True, wait=bwait("E") + [(vc, g_vc)],
                 inc=tc)
            gN = tc.n
            v(lambda e, wi=wi: e.tensor_scalar(
                tcols[0:1, 16 + wi:17 + wi], psE[0:1, 0:1],
                float(1.0 / (FF * D)), float(eps_g), ALU.mult, ALU.add),
              wait=[(tc, gN)], inc=vc)
            bset("E", vc)
        v(lambda e: e.drain())
        v(lambda e: e.reciprocal(tcols[0:1, 18:20], tcols[0:1, 16:18]))
        v(lambda e: e.drain(), inc=vc)
        rg_vc = vc.n
        for wi in range(2):
            t_mm(psE[0:128, wi:wi + 1], orow_f32[0:1, :],
                 tcols[0:1, 18 + wi:19 + wi], start=True, stop=True,
                 wait=bwait("E") + [(vc, rg_vc)],
                 inc=tc if wi == 1 else None)
        rgN = tc.n
        v(lambda e: e.tensor_copy(rgb[:, 0:2], psE[0:128, 0:2]),
          wait=[(tc, rgN)], inc=vc)
        bset("E", vc)
        rgb_vc = vc.n
        if layer == 0:
            dma("sync", lnwc[:, :], ln_w.rearrange("(a p2) -> p2 a", p2=128),
                inc=dLNW, slow=True)
            dma("gpsimd", lnbc_[:, :], ln_b.rearrange("(a p2) -> p2 a", p2=128),
                inc=dLNB, slow=True)
            v(lambda e: e.drain(), wait=[(dLNB, dLNB.n), (dLNW, dLNW.n)])
            v(lambda e: e.reciprocal(tcols[:, 32:64], lnwc[:, :]))
            v(lambda e: e.drain())
            v(lambda e: e.tensor_tensor(lnbcb[:, :], lnbc_[:, :],
                                        tcols[:, 32:64], ALU.mult), inc=vc)

        prev_sp = {}
        last_q_vc = 0
        nd = len(braw)
        act_hist = []
        for r in range(32):
            br = braw[r % nd]
            wprev = [(sc, act_hist[r - nd])] if r >= nd else [(vc, rgb_vc)]
            dma("gpsimd", br[:, :], w1_in[r * 128:(r + 1) * 128, :],
                wait=wprev, inc=dBR[r % nd])
            ldN = (dBR[r % nd], dBR[r % nd].n)
            s(lambda e, br=br: e.activation(
                tful[:, 0:1024], br[:, :], AF.Identity, bias=magic_t[:, 0:1],
                scale=rgb[:, 0:1]),
              wait=[ldN, (dC0, dC0.n), (vc, last_q_vc)], inc=sc)
            sN = sc.n
            act_hist.append(sN)
            v(lambda e: e.tensor_scalar(tful[:, 0:1024], tful[:, 0:1024],
                                        float(MAGIC), 1.0, ALU.subtract,
                                        ALU.min), wait=[(sc, sN)])
            wq = [prev_sp[r % 2], (dW1S, dW1S.n)] if (r % 2) in prev_sp \
                else []
            v(lambda e: e.tensor_scalar(qful[:, 0:1024], tful[:, 0:1024],
                                        -1.0, None, ALU.max),
              wait=wq, inc=vc)
            qN = vc.n
            last_q_vc = qN
            trN = tr8(qful, wait=[(vc, qN)])
            qv4 = qful[:, 1024:2048].rearrange(
                "p (a two c2) -> p a two c2", a=4, two=2)
            v(lambda e: e.nop(), wait=[(dW1S, dW1S.n)])
            evac_pair(qv4[:, :, 0, :], qv4[:, :, 1, :], trN)
            evN = vc.n
            for dt in range(8):
                dma("sync", w1scr[dt * 128:(dt + 1) * 128,
                                  r * 128:(r + 1) * 128],
                    qful[:, 1024 + dt * 128:1024 + (dt + 1) * 128],
                    wait=[(vc, evN)], inc=dW1S)
            prev_sp[r % 2] = (dW1S, dW1S.n)

        act2_hist = []
        for r in range(16):
            br2 = braw2[r % 2]
            dtile = r // 2
            fhalf = r % 2
            wprev = [(sc, act2_hist[r - 2])] if r >= 2 else [(sc, sc.n)]
            dma("gpsimd", br2[:, :],
                w2_in[dtile * 128:(dtile + 1) * 128,
                      fhalf * 2048:(fhalf + 1) * 2048], wait=wprev,
                inc=dB2[r % 2])
            ldN = (dB2[r % 2], dB2[r % 2].n)
            s(lambda e, br2=br2: e.activation(
                tful[:, :], br2[:, :], AF.Identity, bias=magic_t[:, 0:1],
                scale=rgb[:, 1:2]), wait=[ldN, (vc, last_q_vc)], inc=sc)
            sN = sc.n
            act2_hist.append(sN)
            v(lambda e: e.tensor_scalar(tful[:, :], tful[:, :], float(MAGIC),
                                        1.0, ALU.subtract, ALU.min),
              wait=[(sc, sN)])
            v(lambda e: e.tensor_scalar(qful[:, :], tful[:, :], -1.0, None,
                                        ALU.max), wait=[(dW1S, dW1S.n)],
              inc=vc)
            qN = vc.n
            last_q_vc = qN
            for fq in range(2):
                trN = tr8(qful[:, fq * 1024:(fq + 1) * 1024],
                          wait=[(vc, qN)])
                f0 = fhalf * 16 + fq * 8
                if layer == 0:
                    for fk in range(8):
                        bank = ptA if fk % 2 == 0 else ptEb
                        v(lambda e, fk=fk, f0=f0, dtile=dtile, bank=bank:
                          e.tensor_scalar(
                              w2res[:, (f0 + fk) * 1024 + dtile * 128:
                                    (f0 + fk) * 1024 + (dtile + 1) * 128],
                              bank[:, fk * 128:(fk + 1) * 128],
                              lnwc[:, f0 + fk:f0 + fk + 1], None, ALU.mult),
                          wait=([(tc, trN), (dLNW, 16)]
                                if fk == 0 else ()),
                          inc=vc if fk == 7 else None)
                    bset("T", vc)
                    bset("E", vc)
                else:
                    wv = w2res[:, :].rearrange(
                        "p (a two c2) -> p a two c2", a=16, two=2)
                    evac_pair(
                        wv[:, f0 // 2:f0 // 2 + 4, 0,
                           dtile * 128:(dtile + 1) * 128],
                        wv[:, f0 // 2:f0 // 2 + 4, 1,
                           dtile * 128:(dtile + 1) * 128], trN)
        w2_vc = vc.n
        if dbg4_ap is not None and layer == 0:
            for r8 in range(8):
                dma("sync", dbg4_ap[r8 * 128:(r8 + 1) * 128, :],
                    w1scr[r8 * 128:(r8 + 1) * 128, :],
                    wait=[(dW1S, dW1S.n)] if r8 == 0 else (), inc=dDBG)
        ar.release()

        w1sl = [ar.alloc([128, 8 * 512], BF16, f"w1sl{i}") for i in range(2)]
        gT = ar.alloc([128, 32 * 512], BF16, "gT")
        xn_st = ar.alloc([128, D], BF16, "xnstf")
        if region is not None:
            a0 = region[0]
            fscr2 = nc.alloc_sbuf_tensor_at(f"fscr2E{layer}", [128, 1024],
                                            F32, offset=a0)
            xnTf = nc.alloc_sbuf_tensor_at(f"xnTfE{layer}", [128, 8 * 512],
                                           BF16, offset=a0 + 4096)
            ybuf = nc.alloc_sbuf_tensor_at(f"ybufE{layer}", [128, 1024],
                                           BF16, offset=a0 + 12288)
            ubc = nc.alloc_sbuf_tensor_at(f"ubcE{layer}", [128, 1024],
                                          BF16, offset=a0 + 14336)
            bbc = ar.alloc([128, 1024], BF16, f"bbc{layer}")
        else:
            xnTf = ar.alloc([128, 8 * 512], BF16, f"xnTf{layer}")
            ybuf = ar.alloc([128, 1024], BF16, f"ybuf{layer}")
            fscr2 = ar.alloc([128, 1024], F32, f"fscr2{layer}")
            ubc = ar.alloc([128, 1024], BF16, f"ubc{layer}")
            bbc = ar.alloc([128, 1024], BF16, f"bbc{layer}")
        sqst = xn_st

        if layer == 0:
            for row, lh in ((0, None), (1, None)):
                for n2 in range(2):
                    for kt in range(32):
                        lhsT = (ones_bf[:, 0:1] if row == 0
                                else lnbcb[:, kt:kt + 1])
                        t_mm(psE[0:1, 0:512], lhsT,
                             w2res[:, kt * 1024 + n2 * 512:
                                   kt * 1024 + (n2 + 1) * 512],
                             start=(kt == 0), stop=(kt == 31),
                             wait=(bwait("E") + [(vc, w2_vc)])
                             if kt == 0 else (),
                             inc=tc if kt == 31 else None)
                    uN = tc.n
                    v(lambda e, row=row, n2=n2: e.tensor_copy(
                        trow_b[0:1, row * 1024 + n2 * 512:
                               row * 1024 + (n2 + 1) * 512],
                        psE[0:1, 0:512]), wait=[(tc, uN)])
                    v(lambda e: e.drain(), inc=vc)
                    bset("E", vc)
            ub_vc = vc.n
            for row, tgt in ((0, ubc), (1, bbc)):
                for n2 in range(2):
                    t_mm(psB[:, n2 * 512:(n2 + 1) * 512], orow_bf[0:1, :],
                         trow_b[0:1, row * 1024 + n2 * 512:
                                row * 1024 + (n2 + 1) * 512],
                         start=True, stop=True,
                         wait=bwait("B") + [(vc, ub_vc)],
                         inc=tc if n2 == 1 else None)
                ubN = tc.n
                v(lambda e, tgt=tgt: e.tensor_copy(tgt[:, :], psB[:, :]),
                  wait=[(tc, ubN)], inc=vc)
                bset("B", vc)

        for g in ([] if pre_vc_done is not None else range(NT)):
            ln_stats(res_t[:, g * D:(g + 1) * D], D, 0,
                     wait=[(dX, x_fin)] if g == 0 else ())
            rn = rstd_from_var(1, 2)
            if layer == 1:
                v(lambda e: e.tensor_scalar(tcols[:, 3:4], tcols[:, 1:2],
                                            1e-30, None, ALU.add))
                v(lambda e: e.drain(), inc=vc)
                vn2 = vc.n
                s(lambda e: e.activation(tcols[:, 5:6], tcols[:, 3:4],
                                         AF.Sqrt), wait=[(vc, vn2)], inc=sc)
                rn = sc.n
                v(lambda e: e.tensor_tensor(tcols[:, 3:4], tcols[:, 5:6],
                                            tcols[:, 2:3], ALU.mult),
                  wait=[(sc, rn)])
                v(lambda e: e.drain())
                v(lambda e: e.tensor_scalar(tcols[:, 3:4], tcols[:, 3:4],
                                            1e-8, None, ALU.add))
                v(lambda e: e.drain())
                v(lambda e: e.reciprocal(tcols[:, 4:5], tcols[:, 3:4]))
                v(lambda e: e.drain())
                v(lambda e: e.tensor_tensor(tcols[:, 2:3], tcols[:, 2:3],
                                            tcols[:, 4:5], ALU.mult))
                v(lambda e: e.drain())
                rn = 0
            v(lambda e, g=g: e.tensor_copy(precols[:, g * 2:g * 2 + 1],
                                           tcols[:, 0:1]),
              wait=[(sc, rn)] if rn else ())
            v(lambda e, g=g: e.tensor_copy(precols[:, g * 2 + 1:g * 2 + 2],
                                           tcols[:, 2:3]), inc=vc)
        pre_vc = vc.n if pre_vc_done is None else pre_vc_done

        out_dma = 0
        w1_used = {}
        for c in range(8):
            for i in range(4):
                g = c * 4 + i
                v(lambda e, g=g: e.tensor_scalar(
                    xn_st[:, :], res_t[:, g * D:(g + 1) * D],
                    precols[:, g * 2:g * 2 + 1],
                    precols[:, g * 2 + 1:g * 2 + 2],
                    ALU.subtract, ALU.mult), wait=[(vc, pre_vc)], inc=vc)
                xnN = vc.n
                trN = tr8(xn_st, wait=[(vc, xnN)])
                xv4 = xnTf[:, :].rearrange(
                    "p (a two c2) -> p a two c2", a=4, two=2)
                evac_pair(xv4[:, :, 0, i * 128:(i + 1) * 128],
                          xv4[:, :, 1, i * 128:(i + 1) * 128], trN)
            xnT_vc = vc.n

            for fg in range(8):
                wsl = w1sl[fg % 2]
                wprev = ([(tc, w1_used[fg % 2])]
                         if (fg % 2) in w1_used else [(vc, w2_vc)])
                for kt in range(8):
                    dma("sync", wsl[:, kt * 512:(kt + 1) * 512],
                        w1scr[kt * 128:(kt + 1) * 128,
                              fg * 512:(fg + 1) * 512],
                        wait=(wprev + [(dW1S, dW1S.n)])
                        if kt == 0 else (), inc=dW1[fg % 2])
                w1N = (dW1[fg % 2], dW1[fg % 2].n)
                for fi in range(4):
                    ft = fg * 4 + fi
                    ps, bk = (psC, "C") if ft % 2 == 0 else (psD, "D")
                    for kt in range(8):
                        t_mm(ps[:, :],
                             wsl[:, kt * 512 + fi * 128:
                                 kt * 512 + (fi + 1) * 128],
                             xnTf[:, kt * 512:(kt + 1) * 512],
                             start=(kt == 0), stop=(kt == 7),
                             wait=(bwait(bk) + [(vc, xnT_vc), w1N,
                                                (dC0, dC0.n)])
                             if kt == 0 else (),
                             inc=tc if kt == 7 else None)
                    mmN = tc.n
                    s(lambda e, ps=ps, ft=ft: e.activation(
                        gT[:, ft * 512:(ft + 1) * 512], ps[:, :], AF.Gelu),
                      wait=[(tc, mmN)], inc=sc)
                    bset(bk, sc)
                    geN = sc.n
                    v(lambda e, ft=ft: e.tensor_tensor(
                        sqst[:, 0:512], gT[:, ft * 512:(ft + 1) * 512],
                        gT[:, ft * 512:(ft + 1) * 512], ALU.mult),
                      wait=[(sc, geN)], inc=vc)
                    sqN = vc.n
                    if layer == 0:
                        t_mm(psB[0:1, 0:512], ones_bf[:, 0:1],
                             gT[:, ft * 512:(ft + 1) * 512],
                             start=(ft == 0), stop=(ft == 31),
                             wait=(bwait("B") + [(sc, geN)]) if ft == 0
                             else [(sc, geN)])
                    t_mm(psB[0:1, 512:1024] if layer == 0
                         else psB[0:1, 0:512],
                         ones_bf[:, 0:1], sqst[:, 0:512],
                         start=(ft == 0), stop=(ft == 31),
                         wait=(bwait("B") + [(vc, sqN)]) if ft == 0
                         else [(vc, sqN)],
                         inc=tc if ft == 31 else None)
                w1_used[fg % 2] = tc.n
            stN = tc.n
            nr = 2 if layer == 0 else 1
            v(lambda e: e.tensor_copy(trowA[0:1, 0:512], psB[0:1, 0:512]),
              wait=[(tc, stN)])
            if layer == 0:
                v(lambda e: e.tensor_copy(trowB[0:1, 0:512],
                                          psB[0:1, 512:1024]))
            v(lambda e: e.drain(), inc=vc)
            bset("B", vc)
            strN = vc.n
            if dbg2_ap is not None and c == 0 and layer == layer:
                dma("sync", dbg2_ap[0, 0:1, 0:512], trowA[0:1, 0:512],
                    wait=[(vc, strN)], inc=dDBG)
                if layer == 0:
                    dma("sync", dbg2_ap[0, 0:1, 512:1024], trowB[0:1, 0:512],
                        inc=dDBG)
                dma("gpsimd", dbg2_ap[2, :, 0:4096],
                    gT[:, 0:4096], wait=[(vc, strN)], inc=dDBG2)
                dma("gpsimd", dbg2_ap[3, :, 0:4096],
                    xnTf[:, 0:4096], wait=[(vc, strN)], inc=dDBG2)
            ptAf = ptA[:, :].bitcast(F32)
            for i in range(4):
                rows = (trowA,) if layer == 1 else (trowA, trowB)
                for ri, rowt in enumerate(rows):
                    last_tr = (i == 3 and ri == len(rows) - 1)
                    bank = psE if ri == 0 else ptAf
                    p.add("tensor",
                          lambda e, i=i, bank=bank, rowt=rowt: e.transpose(
                              bank[0:128, 16 + i:17 + i],
                              rowt[0:1, i * 128:(i + 1) * 128],
                              id4f_t[0:1, 0:1]),
                          wait=(bwait("E") + bwait("T") + [(vc, strN)])
                          if (i == 0 and ri == 0) else (),
                          inc=tc if last_tr else None)
            scN2 = tc.n
            v(lambda e: e.tensor_copy(tcols[:, 24:28], psE[0:128, 16:20]),
              wait=[(tc, scN2)], inc=vc)
            if layer == 0:
                v(lambda e: e.tensor_copy(tcols[:, 28:32],
                                          ptAf[0:128, 16:20]), inc=vc)
            bset("E", vc)
            bset("T", vc)
            v(lambda e: e.drain())
            if layer == 0:
                v(lambda e: e.tensor_scalar(tcols[:, 48:52], tcols[:, 24:28],
                                            float(1.0 / FF), None, ALU.mult))
                v(lambda e: e.tensor_scalar(tcols[:, 52:56], tcols[:, 28:32],
                                            float(1.0 / FF), float(LN_EPS),
                                            ALU.mult, ALU.add))
                v(lambda e: e.tensor_tensor(tcols[:, 56:60], tcols[:, 48:52],
                                            tcols[:, 48:52], ALU.mult))
                v(lambda e: e.drain())
                v(lambda e: e.tensor_tensor(tcols[:, 44:48], tcols[:, 52:56],
                                            tcols[:, 56:60], ALU.subtract))
                v(lambda e: e.drain())
                v(lambda e: e.reciprocal(tcols[:, 56:60], tcols[:, 44:48]))
                v(lambda e: e.drain(), inc=vc)
                vb = vc.n
                s(lambda e: e.activation(tcols[:, 52:56], tcols[:, 56:60],
                                         AF.Sqrt), wait=[(vc, vb)], inc=sc)
                rstd_sc = sc.n
                v(lambda e: e.tensor_tensor(tcols[:, 56:60], tcols[:, 48:52],
                                            tcols[:, 52:56], ALU.mult),
                  wait=[(sc, rstd_sc)], inc=vc)
            else:
                v(lambda e: e.tensor_scalar(tcols[:, 48:52], tcols[:, 24:28],
                                            1e-30, None, ALU.add))
                v(lambda e: e.drain(), inc=vc)
                vb = vc.n
                s(lambda e: e.activation(tcols[:, 52:56], tcols[:, 48:52],
                                         AF.Sqrt), wait=[(vc, vb)], inc=sc)
                rstd_sc = sc.n
                _rstd_ln_sc[0] = rstd_sc
                v(lambda e: e.tensor_scalar(tcols[:, 48:52], tcols[:, 52:56],
                                            float(1.0 / 64.0), 1e-8,
                                            ALU.mult, ALU.add),
                  wait=[(sc, rstd_sc)])
                v(lambda e: e.drain())
                v(lambda e: e.reciprocal(tcols[:, 52:56], tcols[:, 48:52]))
                v(lambda e: e.drain(), inc=vc)
            sc_vc = vc.n
            if dbg2_ap is not None and c == 0:
                dma("sync", dbg2_ap[0, :, 1024:1064], tcols[:, 24:64],
                    wait=[(vc, sc_vc)], inc=dDBG)
            for i in range(4):
                g = c * 4 + i
                for n2 in range(2):
                    for kt in range(32):
                        t_mm(psA[:, n2 * 512:(n2 + 1) * 512],
                             gT[:, kt * 512 + i * 128:
                                kt * 512 + (i + 1) * 128],
                             w2res[:, kt * 1024 + n2 * 512:
                                   kt * 1024 + (n2 + 1) * 512],
                             start=(kt == 0), stop=(kt == 31),
                             wait=(bwait("A") + [(vc, sc_vc)])
                             if kt == 0 else (),
                             inc=tc if kt == 31 else None)
                mmN2 = tc.n
                if layer == 0:
                    rs_c, c2_c = 52 + i, 56 + i
                    wout = [(dOUT, out_dma)] if out_dma else []
                    v(lambda e, c2_c=c2_c: e.tensor_scalar(
                        tcols[:, 14:15], tcols[:, c2_c:c2_c + 1], -1.0,
                        None, ALU.mult), wait=wout + [(tc, mmN2)])
                    v(lambda e: e.drain())
                    v(lambda e: e.scalar_tensor_tensor(
                        fscr2[:, :], ubc[:, :], tcols[:, 14:15], bbc[:, :],
                        ALU.mult, ALU.add))
                    v(lambda e, rs_c=rs_c: e.scalar_tensor_tensor(
                        ybuf[:, :], psA[:, :], tcols[:, rs_c:rs_c + 1],
                        fscr2[:, :], ALU.mult, ALU.add), inc=vc)
                else:
                    v(lambda e, rs_c=52 + i: e.tensor_scalar(
                        ybuf[:, :], psA[:, :], tcols[:, rs_c:rs_c + 1],
                        None, ALU.mult), wait=[(tc, mmN2)], inc=vc)
                bset("A", vc)
                yv = vc.n
                if dbg2_ap is not None and c == 0 and i == 0:
                    dma("gpsimd", dbg2_ap[1, :, 0:1024], ybuf[:, :],
                        wait=[(vc, yv)], inc=dDBG2)
                    if layer == 0:
                        dma("gpsimd", dbg2_ap[1, :, 1024:2048], ubc[:, :],
                            inc=dDBG2)
                        dma("gpsimd", dbg2_ap[1, :, 2048:3072], bbc[:, :],
                            inc=dDBG2)
                ln_stats(ybuf, D, 4, wait=[(vc, yv)])
                rn3 = rstd_from_var(5, 6)
                wout = [(dOUT, out_dma)] if out_dma else []
                v(lambda e: e.tensor_scalar(
                    fscr2[:, :], ybuf[:, :], tcols[:, 4:5], tcols[:, 6:7],
                    ALU.subtract, ALU.mult), wait=wout + [(sc, rn3)],
                  inc=vc)
                if layer == 0:
                    v(lambda e, g=g: e.tensor_tensor(
                        res_t[:, g * D:(g + 1) * D],
                        res_t[:, g * D:(g + 1) * D], fscr2[:, :], ALU.add),
                      inc=vc)
                else:
                    v(lambda e, g=g: e.tensor_tensor(
                        fscr2[:, :], fscr2[:, :],
                        res_t[:, g * D:(g + 1) * D], ALU.add), inc=vc)
                    oN = vc.n
                    dma("sync", out_ap[g * 128:(g + 1) * 128, :],
                        fscr2[:, :], wait=[(vc, oN)], inc=dOUT)
                    out_dma = dOUT.n
        ar.release()

    if "l0ffn" in phases:
        emit_ffn(0, early=early, region=eb_addr)

    if "l1attn" in phases:
        if pair1 and eblock_on:
            early1 = mk_pump(w_up, w_dn, "eacc1")
            early1["start"]()
        ar.mark()
        wqT = ar.alloc([128, 8 * 1024], BF16, "wqT")
        wvT = ar.alloc([128, 8 * 1024], BF16, "wvT")
        wgT = ar.alloc([128, 8 * 1024], BF16, "wgT")
        woT = ar.alloc([128, 8 * 1024], BF16, "woT")
        LT = ar.alloc([128, 16 * 128], BF16, "LT")
        pexp = ar.alloc([1, 2048], F32, "pexp")
        lnlam = ar.alloc([1, 16], F32, "lnlam")
        decs = ar.alloc([1, 16], F32, "decs")
        tri01_t = ar.alloc([128, 128], BF16, "tri01")
        gamb = ar.alloc([128, 1024], BF16, "gamb")
        betb = ar.alloc([128, 1024], BF16, "betb")
        xnT1a = ar.alloc([128, 8 * 128], BF16, "xnT1a")
        xn_st1 = ar.alloc([128, D], BF16, "xnst1")
        qst1 = ar.alloc([128, 1024], BF16, "qst1")
        vstla = ar.alloc([128, 1024], BF16, "vstla")
        gsta = ar.alloc([128, 1024], BF16, "gsta")
        yst1 = ar.alloc([128, 1024], F32, "yst1")
        ygba = ar.alloc([128, 1024], BF16, "ygba")
        ygTa = ar.alloc([128, 8 * 128], BF16, "ygTa")
        carry = ar.alloc([1, 1024], F32, "carry")
        lamrev = ar.alloc([128, 16], BF16, "lamrev")
        lam128 = ar.alloc([1, 16], F32, "lam128")
        fscr3 = ar.alloc([128, 1024], F32, "fscr3")
        ar.mark()
        prow = ar.alloc([1, 2048], BF16, "prow")
        nrow = ar.alloc([1, 2048], BF16, "nrow")
        rowf = ar.alloc([1, 1024], F32, "rowf")
        wrawb = [ar.alloc([128, 1024], BF16, f"wrawb{i}") for i in range(2)]

        dWR = [p.ctr("dWRa"), p.ctr("dWRb")]
        dDEC = p.ctr("dDEC")
        dGBr = p.ctr("dGBr")
        dTRI = p.ctr("dTRI")
        l1_guard = [(tc, tc.n), (vc, vc.n), (sc, sc.n)]
        dma("sync", tri01_t[:, :], c_tri01[:, :], wait=l1_guard, inc=dTRI)

        widx = 0
        ev_hist = []
        for wsrc, wdst in ([] if "w" in L1P_SKIP else
                           ((w_q1, wqT), (w_v1, wvT), (w_g1, wgT),
                            (w_o1, woT))):
            for r in range(8):
                wb = wrawb[widx % 2]
                wprev = [(vc, ev_hist[widx - 2])] if widx >= 2 \
                    else l1_guard
                dma("gpsimd", wb[:, :], wsrc[r * 128:(r + 1) * 128, :],
                    wait=wprev, inc=dWR[widx % 2])
                ldN = (dWR[widx % 2], dWR[widx % 2].n)
                trN = tr8(wb, wait=[ldN, (dC0, dC0.n)])
                wv4 = wdst[:, :].rearrange(
                    "p (a two c2) -> p a two c2", a=4, two=2)
                evac_pair(wv4[:, :, 0, r * 128:(r + 1) * 128],
                          wv4[:, :, 1, r * 128:(r + 1) * 128], trN)
                ev_hist.append(vc.n)
                widx += 1
        wT_vc = vc.n

        if "lam" in L1P_SKIP:
            v(lambda e: e.memset(pexp[0:1, :], 0.5), inc=vc)
            v(lambda e: e.memset(prow[0:1, :], 0.5))
            v(lambda e: e.memset(nrow[0:1, :], 0.5))
            v(lambda e: e.memset(lnlam[0:1, :], 0.5), inc=vc)
        dma("gpsimd", decs[0:1, :], dec_in.unsqueeze(0), wait=l1_guard,
            inc=dDEC)
        s(lambda e: e.activation(lnlam[0:1, :], decs[0:1, :], AF.Ln),
          wait=[(dDEC, 16)], inc=sc)
        ln_sc = sc.n
        for hf in ([] if "lam" in L1P_SKIP else range(2)):
            for hh in range(8):
                h = hf * 8 + hh
                v(lambda e, hh=hh, h=h: e.tensor_scalar(
                    rowf[0:1, hh * 128:(hh + 1) * 128], iota_t[0:1, :],
                    lnlam[0:1, h:h + 1], None, ALU.mult),
                  wait=[(sc, sc.n)])
            v(lambda e: e.drain(), inc=vc)
            a_vc = vc.n
            s(lambda e, hf=hf: e.activation(
                prow[0:1, hf * 1024:(hf + 1) * 1024], rowf[0:1, :], AF.Exp),
              wait=[(vc, a_vc)])
            s(lambda e, hf=hf: e.activation(
                nrow[0:1, hf * 1024:(hf + 1) * 1024], rowf[0:1, :], AF.Exp,
                scale=-1.0), inc=sc)
            pn_sc = sc.n
            for hh in range(8):
                h = hf * 8 + hh
                v(lambda e, hh=hh, h=h: e.tensor_scalar(
                    rowf[0:1, hh * 128:(hh + 1) * 128], iota1_t[0:1, :],
                    lnlam[0:1, h:h + 1], None, ALU.mult),
                  wait=[(sc, pn_sc)])
            v(lambda e: e.drain(), inc=vc)
            a2_vc = vc.n
            s(lambda e, hf=hf: e.activation(
                pexp[0:1, hf * 1024:(hf + 1) * 1024], rowf[0:1, :], AF.Exp),
              wait=[(vc, a2_vc)], inc=sc)
        pe_sc = sc.n
        for h in ([] if "lt" in L1P_SKIP else range(NH)):
            t_mm(psE[0:128, 0:128], nrow[0:1, h * 128:(h + 1) * 128],
                 prow[0:1, h * 128:(h + 1) * 128], start=True, stop=True,
                 wait=bwait("E") + [(sc, pe_sc), (dC0, dC0.n)], inc=tc)
            lN = tc.n
            v(lambda e, h=h: e.tensor_tensor(
                LT[:, h * 128:(h + 1) * 128], psE[0:128, 0:128],
                tri01_t[:, :], ALU.mult), wait=[(tc, lN), (dTRI, 16)],
              inc=vc)
            bset("E", vc)
        LT_vc = vc.n
        if "lt" in L1P_SKIP:
            v(lambda e: e.memset(LT[:, :], 0.25), inc=vc)
            LT_vc = vc.n
        for hf in ([] if "lr" in L1P_SKIP else range(2)):
            for hh in range(8):
                h = hf * 8 + hh
                v(lambda e, hh=hh, h=h: e.tensor_scalar(
                    rowf[0:1, hh * 128:(hh + 1) * 128], iotar_t[0:1, :],
                    lnlam[0:1, h:h + 1], None, ALU.mult),
                  wait=[(sc, sc.n)])
            v(lambda e: e.drain(), inc=vc)
            lrv = vc.n
            s(lambda e, hf=hf: e.activation(
                prow[0:1, hf * 1024:(hf + 1) * 1024], rowf[0:1, :], AF.Exp),
              wait=[(vc, lrv)], inc=sc)
            lrs = sc.n
            v(lambda e, hf=hf: e.tensor_copy(
                nrow[0:1, hf * 1024:(hf + 1) * 1024],
                prow[0:1, hf * 1024:(hf + 1) * 1024]),
              wait=[(sc, lrs)], inc=vc)
        v(lambda e: e.drain(), inc=vc)
        lrb_vc = vc.n
        for h in ([] if "lr" in L1P_SKIP else range(NH)):
            bank = ptA if h % 2 == 0 else ptEb
            p.add("tensor", lambda e, h=h, bank=bank: e.transpose(
                bank[:, 2 * (h // 2):2 * (h // 2) + 1],
                nrow[0:1, h * 128:(h + 1) * 128],
                ident_t[0:1, 0:1]),
                wait=(bwait("T") + bwait("E") + [(vc, lrb_vc)])
                if h == 0 else (),
                inc=tc if h == NH - 1 else None)
            p.add("tensor", lambda e: e.nop())
            p.add("tensor", lambda e: e.nop())
        lrt = tc.n
        if "lr" in L1P_SKIP:
            v(lambda e: e.memset(lamrev[:, :], 0.25), inc=vc)
        else:
            v(lambda e: e.tensor_copy(
                lamrev[:, 0:16].rearrange("p (a two) -> p a two", two=2)
                [:, :, 0:1],
                ptA[:, 0:16].rearrange("p (a two) -> p a two", two=2)
                [:, :, 0:1]),
              wait=[(tc, lrt)])
            v(lambda e: e.tensor_copy(
                lamrev[:, 0:16].rearrange("p (a two) -> p a two", two=2)
                [:, :, 1:2],
                ptEb[:, 0:16].rearrange("p (a two) -> p a two", two=2)
                [:, :, 0:1]))
            v(lambda e: e.drain(), inc=vc)
            bset("T", vc)
            bset("E", vc)
        v(lambda e: e.tensor_scalar(lam128[0:1, :], lnlam[0:1, :], 128.0,
                                    None, ALU.mult))
        v(lambda e: e.drain(), inc=vc)
        l128v = vc.n
        s(lambda e: e.activation(lam128[0:1, :], lam128[0:1, :], AF.Exp),
          wait=[(vc, l128v)], inc=sc)
        lam_sc = sc.n

        dma("sync", fscr3[0:1, :], gam1.unsqueeze(0), wait=l1_guard,
            inc=dGBr)
        dma("sync", carry[0:1, :], bet1.unsqueeze(0), inc=dGBr)
        gb_fin = dGBr.n
        if "gb" in L1P_SKIP:
            v(lambda e: e.memset(gamb[:, :], 1.0), inc=vc)
            v(lambda e: e.memset(betb[:, :], 0.0), inc=vc)
        for rowsrc, tgt in ([] if "gb" in L1P_SKIP else
                            ((fscr3, gamb), (carry, betb))):
            for n2 in range(2):
                t_mm(psB[:, n2 * 512:(n2 + 1) * 512], orow_f32[0:1, :],
                     rowsrc[0:1, n2 * 512:(n2 + 1) * 512],
                     start=True, stop=True,
                     wait=bwait("B") + [(dGBr, gb_fin)],
                     inc=tc if n2 == 1 else None)
            gbN = tc.n
            v(lambda e, tgt=tgt: e.tensor_copy(tgt[:, :], psB[:, :]),
              wait=[(tc, gbN)], inc=vc)
            bset("B", vc)
        gb_vc = vc.n

        for g in range(NT):
            ln_stats(res_t[:, g * D:(g + 1) * D], D, 0,
                     wait=[(dX, x_fin)] if g == 0 else ())
            rn = rstd_from_var(1, 2)
            v(lambda e, g=g: e.tensor_copy(precols[:, g * 2:g * 2 + 1],
                                           tcols[:, 0:1]), wait=[(sc, rn)])
            v(lambda e, g=g: e.tensor_copy(precols[:, g * 2 + 1:g * 2 + 2],
                                           tcols[:, 2:3]), inc=vc)
        pre1_vc = vc.n
        ar.release()

        xnT1 = [xnT1a, ar.alloc([128, 8 * 128], BF16, "xnT1b")]
        qst = [qst1, ar.alloc([128, 1024], BF16, "qstb")]
        vst_ = [vstla, ar.alloc([128, 1024], BF16, "vstb")]
        gst = [gsta, ar.alloc([128, 1024], BF16, "gstb")]
        ygb = [ygba, ar.alloc([128, 1024], BF16, "ygbb")]
        ygT = [ygTa, ar.alloc([128, 8 * 128], BF16, "ygTb")]
        yst = [yst1, ar.alloc([128, 1024], F32, "ystb")]

        pipe = (L1_LIM is None and L1_STAGE >= 99 and not L1_SKIP_SCAN
                and not L1_SKIP_CARRY and not L1_SKIP_QV
                and not L1_SKIP_TR)
        base_w = max(pre1_vc, wT_vc, LT_vc, gb_vc)

        def stage_A(g, pp):
            v(lambda e, g=g: e.tensor_scalar(
                xn_st1[:, :], res_t[:, g * D:(g + 1) * D],
                precols[:, g * 2:g * 2 + 1],
                precols[:, g * 2 + 1:g * 2 + 2],
                ALU.subtract, ALU.mult),
              wait=[(vc, base_w)], inc=vc)
            xnN = vc.n
            trN = tr8(xn_st1, wait=[(vc, xnN)])
            xt = xnT1[pp]
            xt4 = xt[:, :].rearrange("p (a two c2) -> p a two c2",
                                     a=4, two=2)
            evac_pair(xt4[:, :, 0, :], xt4[:, :, 1, :], trN)
            return vc.n

        pend_xv = [stage_A(0, 0)] if pipe else [0]
        last_sil = [0]
        _l1cnt = 0
        for b in range(B):
            for ti in range(16):
                if L1_LIM is not None and _l1cnt >= L1_LIM:
                    continue
                _l1cnt += 1
                g = b * 16 + ti
                pp = ti % 2
                xt = xnT1[pp]
                if pipe:
                    xv = pend_xv[0]
                else:
                    v(lambda e, g=g: e.tensor_scalar(
                        xn_st1[:, :], res_t[:, g * D:(g + 1) * D],
                        precols[:, g * 2:g * 2 + 1],
                        precols[:, g * 2 + 1:g * 2 + 2],
                        ALU.subtract, ALU.mult),
                      wait=[(vc, base_w)], inc=vc)
                    xnN = vc.n
                    if not L1_SKIP_TR:
                        trN = tr8(xn_st1, wait=[(vc, xnN)])
                        xt4 = xt[:, :].rearrange(
                            "p (a two c2) -> p a two c2", a=4, two=2)
                        evac_pair(xt4[:, :, 0, :], xt4[:, :, 1, :], trN)
                    else:
                        v(lambda e, xt=xt: e.memset(xt[:, :], 0.5), inc=vc)
                    xv = vc.n
                if L1_STAGE <= 1:
                    continue

                for n2 in ([] if L1_SKIP_QV else range(2)):
                    for kt in range(8):
                        t_mm(psA[:, n2 * 512:(n2 + 1) * 512],
                             xt[:, kt * 128:(kt + 1) * 128],
                             wqT[:, kt * 1024 + n2 * 512:
                                 kt * 1024 + (n2 + 1) * 512],
                             start=(kt == 0), stop=(kt == 7),
                             wait=(bwait("A") + [(vc, xv)])
                             if (kt == 0 and n2 == 0) else (),
                             inc=tc if kt == 7 else None)
                qN = tc.n
                if not L1_SKIP_QV:
                    v(lambda e, q=qst[pp]: e.tensor_copy(q[:, :], psA[:, :]),
                      wait=[(tc, qN)], inc=vc)
                    bset("A", vc)
                for n2 in ([] if L1_SKIP_QV else range(2)):
                    for kt in range(8):
                        t_mm(psB[:, n2 * 512:(n2 + 1) * 512],
                             xt[:, kt * 128:(kt + 1) * 128],
                             wvT[:, kt * 1024 + n2 * 512:
                                 kt * 1024 + (n2 + 1) * 512],
                             start=(kt == 0), stop=(kt == 7),
                             wait=bwait("B") if (kt == 0 and n2 == 0)
                             else (), inc=tc if kt == 7 else None)
                vN = tc.n
                if not L1_SKIP_QV:
                    v(lambda e, vt=vst_[pp]: e.tensor_copy(
                        vt[:, :], psB[:, :]), wait=[(tc, vN)], inc=vc)
                    bset("B", vc)
                else:
                    v(lambda e, vt=vst_[pp]: e.memset(vt[:, :], 1.0),
                      inc=vc)
                vv = vc.n
                if L1_STAGE <= 2:
                    continue
                if pipe and g < B * 16 - 1:
                    pend_xv[0] = stage_A(g + 1, (ti + 1) % 2)

                first = (ti == 0)
                for n2 in ([] if L1_SKIP_SCAN else range(2)):
                    ps, bk = (psC, "C") if n2 == 0 else (psD, "D")
                    for hh in range(8):
                        h = n2 * 8 + hh
                        t_mm(ps[:, hh * 64:(hh + 1) * 64],
                             LT[:, h * 128:(h + 1) * 128],
                             vst_[pp][:, h * 64:(h + 1) * 64],
                             start=(hh == 0),
                             stop=(first and hh == 7),
                             wait=(bwait(bk) + [(vc, vv)])
                             if hh == 0 else (),
                             inc=tc if (first and hh == 7) else None)
                    if not first:
                        for hh in range(8):
                            h = n2 * 8 + hh
                            t_mm(ps[:, hh * 64:(hh + 1) * 64],
                                 pexp[0:1, h * 128:(h + 1) * 128],
                                 carry[0:1, h * 64:(h + 1) * 64],
                                 start=False, stop=(hh == 7),
                                 inc=tc if hh == 7 else None)
                scanN = tc.n
                for n2 in ([] if L1_SKIP_CARRY else
                           ([0] if L1_CARRY_MODE == "n2_0" else range(2))):
                    for hh in range(8):
                        h = n2 * 8 + hh
                        t_mm((psE if L1_CARRY_OUT == "E" else
                              psC)[0:1, hh * 64:(hh + 1) * 64],
                             ones_bf[:, 0:1] if L1_CARRY_MODE == "lhs_ones"
                             else lamrev[:, h:h + 1],
                             vst_[pp][:, h * 64:(h + 1) * 64],
                             start=(hh == 0), stop=(first and hh == 7),
                             wait=(bwait("E") + [(sc, lam_sc),
                                                 (vc, vv)])
                             if hh == 0 else (),
                             inc=tc if (first and hh == 7) else None)
                    if not first:
                        for hh in range(8):
                            h = n2 * 8 + hh
                            t_mm(psE[0:1, hh * 64:(hh + 1) * 64],
                                 lam128[0:1, h:h + 1],
                                 carry[0:1, h * 64:(h + 1) * 64],
                                 start=False, stop=(hh == 7),
                                 inc=tc if hh == 7 else None)
                    cuN = tc.n
                    if L1_CARRY_MODE == "mm_only":
                        v(lambda e: e.nop(), wait=[(tc, cuN)], inc=vc)
                        bset("E", vc)
                        continue
                    v(lambda e, n2=n2: e.tensor_copy(
                        carry[0:1, n2 * 512:(n2 + 1) * 512],
                        psE[0:1, 0:512]), wait=[(tc, cuN)])
                    v(lambda e: e.drain(), inc=vc)
                    bset("E", vc)
                for n2 in ([] if L1_SKIP_SCAN else range(2)):
                    ps = psC if n2 == 0 else psD
                    v(lambda e, ps=ps, n2=n2, q=qst[pp], y=yst[pp]:
                      e.tensor_tensor(y[:, n2 * 512:(n2 + 1) * 512],
                                      q[:, n2 * 512:(n2 + 1) * 512],
                                      ps[:, :], ALU.mult),
                      wait=[(tc, scanN)], inc=vc)
                bset("C", vc)
                bset("D", vc)
                yv1 = vc.n
                if L1_STAGE <= 3:
                    continue

                for n2 in range(2):
                    for kt in range(8):
                        t_mm(psE[:, :], xt[:, kt * 128:(kt + 1) * 128],
                             wgT[:, kt * 1024 + n2 * 512:
                                 kt * 1024 + (n2 + 1) * 512],
                             start=(kt == 0), stop=(kt == 7),
                             wait=bwait("E") if kt == 0 else (),
                             inc=tc if kt == 7 else None)
                    gN = tc.n
                    v(lambda e, n2=n2, gs=gst[pp]: e.tensor_copy(
                        gs[:, n2 * 512:(n2 + 1) * 512], psE[:, :]),
                      wait=[(tc, gN), (sc, last_sil[0])], inc=vc)
                    bset("E", vc)
                gv = vc.n

                ln_stats(yst[pp], D, 8, wait=[(vc, yv1)])
                rn = rstd_from_var(9, 10)
                s(lambda e, gs=gst[pp]: e.activation(gs[:, :], gs[:, :],
                                                     AF.Silu),
                  wait=[(vc, gv)], inc=sc)
                sil = sc.n
                last_sil[0] = sil
                v(lambda e, y=yst[pp]: e.tensor_scalar(
                    fscr3[:, :], y[:, :], tcols[:, 8:9], tcols[:, 10:11],
                    ALU.subtract, ALU.mult), wait=[(sc, max(rn, sil))])
                v(lambda e: e.tensor_tensor(fscr3[:, :], fscr3[:, :],
                                            gamb[:, :], ALU.mult))
                v(lambda e: e.tensor_tensor(fscr3[:, :], fscr3[:, :],
                                            betb[:, :], ALU.add))
                v(lambda e, gs=gst[pp], yg=ygb[pp]: e.tensor_tensor(
                    yg[:, :], fscr3[:, :], gs[:, :], ALU.mult), inc=vc)
                ygv = vc.n
                tr2 = tr8(ygb[pp], wait=[(vc, ygv)])
                yt4 = ygT[pp][:, :].rearrange(
                    "p (a two c2) -> p a two c2", a=4, two=2)
                evac_pair(yt4[:, :, 0, :], yt4[:, :, 1, :], tr2)
                ytv = vc.n
                if L1_STAGE <= 4:
                    continue

                for n2 in range(2):
                    for kt in range(8):
                        t_mm(psA[:, n2 * 512:(n2 + 1) * 512],
                             ygT[pp][:, kt * 128:(kt + 1) * 128],
                             woT[:, kt * 1024 + n2 * 512:
                                 kt * 1024 + (n2 + 1) * 512],
                             start=(kt == 0), stop=(kt == 7),
                             wait=(bwait("A") + [(vc, ytv)])
                             if (kt == 0 and n2 == 0) else (),
                             inc=tc if kt == 7 else None)
                oN = tc.n
                ln_stats(psA, D, 12, wait=[(tc, oN)], lead=False)
                rn = rstd_from_var(13, 14)
                v(lambda e: e.tensor_scalar(
                    fscr3[:, :], psA[:, :], tcols[:, 12:13],
                    tcols[:, 14:15], ALU.subtract, ALU.mult),
                  wait=[(sc, rn)], inc=vc)
                bset("A", vc)
                v(lambda e, g=g: e.tensor_tensor(
                    res_t[:, g * D:(g + 1) * D],
                    res_t[:, g * D:(g + 1) * D], fscr3[:, :], ALU.add),
                  inc=vc)
                if early1 is not None and ti % 2 == 0:
                    early1["pump"]()
                if pipe and "l1ffn" in phases:
                    ln_stats(res_t[:, g * D:(g + 1) * D], D, 0)
                    rstd_from_var(1, 2)
                    v(lambda e: e.tensor_scalar(tcols[:, 3:4],
                                                tcols[:, 1:2],
                                                1e-30, None, ALU.add))
                    v(lambda e: e.drain(), inc=vc)
                    vn2 = vc.n
                    s(lambda e: e.activation(tcols[:, 5:6], tcols[:, 3:4],
                                             AF.Sqrt), wait=[(vc, vn2)],
                      inc=sc)
                    rn2 = sc.n
                    v(lambda e: e.tensor_tensor(tcols[:, 3:4],
                                                tcols[:, 5:6],
                                                tcols[:, 2:3], ALU.mult),
                      wait=[(sc, rn2)])
                    v(lambda e: e.drain())
                    v(lambda e: e.tensor_scalar(tcols[:, 3:4],
                                                tcols[:, 3:4],
                                                1e-8, None, ALU.add))
                    v(lambda e: e.drain())
                    v(lambda e: e.reciprocal(tcols[:, 4:5], tcols[:, 3:4]))
                    v(lambda e: e.drain())
                    v(lambda e: e.tensor_tensor(tcols[:, 2:3],
                                                tcols[:, 2:3],
                                                tcols[:, 4:5], ALU.mult))
                    v(lambda e: e.drain())
                    v(lambda e, g=g: e.tensor_copy(
                        precols[:, g * 2:g * 2 + 1], tcols[:, 0:1]))
                    v(lambda e, g=g: e.tensor_copy(
                        precols[:, g * 2 + 1:g * 2 + 2], tcols[:, 2:3]),
                      inc=vc)
        if early1 is not None:
            early1["pump"]()
        l1_pre_vc = vc.n if (pipe and "l1ffn" in phases) else None
        ar.release()

    if "l1ffn" in phases:
        _pd = l1_pre_vc if "l1attn" in phases else None
        emit_ffn(1, early=early1, region=eb_addr, pre_vc_done=_pd)

    if resd_ap is not None:
        ar.mark()
        rd = ar.alloc([128, 1024], F32, "rdump")
        prev_rd = 0
        for g in range(NT):
            v(lambda e, g=g: e.tensor_copy(rd[:, :],
                                           res_t[:, g * D:(g + 1) * D]),
              wait=[(dDBG, prev_rd), (dX, x_fin)], inc=vc)
            rv = vc.n
            dma("sync", resd_ap[g * 128:(g + 1) * 128, :], rd[:, :],
                wait=[(vc, rv)], inc=dDBG)
            prev_rd = dDBG.n
        ar.release()

    p.add("sync", None, wait=[(dOUT, dOUT.n), (dDBG, dDBG.n),
                              (dDBG2, dDBG2.n)])
    with nc.allow_low_precision(reason="bf16 operands by design; fp32 accum"):
        p.emit()
    return nc, ctx


def shard_inputs(inputs, phases=ALL_PHASES, dump_res=False):
    consts = host_constants()
    x = np.asarray(inputs["L0_attn_x"], dtype=np.float32)
    in_maps = []
    for core in range(NCORES):
        m = dict(consts)
        m["x"] = np.ascontiguousarray(
            x[core * B:(core + 1) * B].reshape(B * T, D))
        for k, val in inputs.items():
            if k != "L0_attn_x":
                m[k] = np.ascontiguousarray(np.asarray(val, dtype=np.float32))
        in_maps.append(m)
    return in_maps


def kernel(**inputs):
    nc, ctx = build_nc()
    in_maps = shard_inputs(inputs)
    res = run_bass_kernel_spmd(nc, in_maps, core_ids=list(range(NCORES)))
    ctx.close()
    out = np.concatenate(
        [np.asarray(r["out"]).reshape(B, T, D) for r in res.results], axis=0)
    return out.astype(np.float32)


if __name__ == "__main__":
    import time
    t0 = time.time()
    nc, ctx = build_nc()
    print(f"built in {time.time()-t0:.1f}s")
